# version 37
# baseline (speedup 1.0000x reference)
import contextlib
import numpy as np
import ml_dtypes
import concourse.bass as bass
import concourse.mybir as mybir
from concourse.bass_utils import run_bass_kernel_spmd

F32 = mybir.dt.float32
BF16 = mybir.dt.bfloat16
I32 = mybir.dt.int32
AF = mybir.ActivationFunctionType
ALU = mybir.AluOpType

NCORES = 8
SEQN = 2
T = 2048
D = 1024
H = 16
DH = 64
FF = 3072
NF = 8
INV_D = 1.0 / float(D)
EPS = 1e-5
EXP_SHIFT = -20.0
QSCALE = 0.125
NEG_LN127 = -float(np.log(127.0))

DMAQ = ("DSP", "DGP")
ENG_OF = {"PE": "tensor", "ACT": "scalar", "DVE": "vector",
          "DSP": "sync", "DGP": "gpsimd", "GP": "gpsimd"}


class Rec:

    def __init__(self):
        self.ops = []
        self.dbg = []
        self.cnt = {}
        self.last_w = {}
        self.readers = {}

    def op(self, q, emit, reads=(), writes=()):
        idx = self.cnt.get(q, 0)
        self.cnt[q] = idx + 1
        deps = set()
        for r in reads:
            w = self.last_w.get(r)
            if w is not None:
                deps.add(w)
        for r in writes:
            w = self.last_w.get(r)
            if w is not None:
                deps.add(w)
            rd = self.readers.get(r)
            if rd:
                deps.update(rd.items())
                self.readers[r] = {}
        for r in reads:
            rd = self.readers.setdefault(r, {})
            if rd.get(q, -1) < idx:
                rd[q] = idx
        for r in writes:
            self.last_w[r] = (q, idx)
        deps = frozenset(d for d in deps if d[0] != q)
        self.ops.append((q, emit, deps, idx))
        self.dbg.append((q, idx, tuple(reads), tuple(writes), deps))
        return (q, idx)


def _emit_program(nc, rec):
    with contextlib.ExitStack() as stk:
        sems = {}
        for q in ("PE", "ACT", "DVE", "DSP", "DGP"):
            sems[q] = stk.enter_context(nc.semaphore("sem_" + q))
        marked = {}
        for q, emit, deps, idx in rec.ops:
            for (q2, i2) in deps:
                if q2 not in DMAQ:
                    marked.setdefault(q2, set()).add(i2)
        rank = {q: {i: r + 1 for r, i in enumerate(sorted(s))}
                for q, s in marked.items()}
        per_eng = {"tensor": [], "scalar": [], "vector": [], "sync": [], "gpsimd": []}
        for rec_op in rec.ops:
            per_eng[ENG_OF[rec_op[0]]].append(rec_op)

        def mk_body(ops_list):
            def body(eng):
                waited = {}
                for q, emit, deps, idx in ops_list:
                    for (q2, i2) in sorted(deps):
                        if q2 in DMAQ:
                            val = 16 * (i2 + 1)
                        else:
                            val = rank[q2][i2]
                        if waited.get(q2, 0) < val:
                            eng.wait_ge(sems[q2], val)
                            waited[q2] = val
                    inst = emit(eng)
                    if q in DMAQ:
                        inst.then_inc(sems[q], 16)
                    elif idx in marked.get(q, ()):
                        inst.then_inc(sems[q], 1)
            return body

        block = stk.enter_context(nc.Block())
        block.tensor(mk_body(per_eng["tensor"]))
        block.scalar(mk_body(per_eng["scalar"]))
        block.vector(mk_body(per_eng["vector"]))
        block.sync(mk_body(per_eng["sync"]))
        block.gpsimd(mk_body(per_eng["gpsimd"]))


def _scr_keys(c0, c1):
    return tuple(f"S{i}" for i in range(c0 // 512, (c1 + 511) // 512))


R3K = tuple(f"R3:{h}" for h in range(H))
R4K = ("vaug", "w40", "w41")



def _wo_tile(fo, kc):
    if fo < 3:
        t = fo * 8 + kc
    elif fo == 3:
        t = 24 + kc if kc < 7 else 63
    else:
        t = 31 + (fo - 4) * 8 + kc
    return t * 128


def build_nc(taps=(), stop_after=None):
    nc = bass.Bass("TRN2", target_bir_lowering=False)
    xt = nc.dram_tensor("xt", [SEQN, D, T], F32, kind="ExternalInput").ap()
    wqkt = nc.dram_tensor("wqkt", [D, 2048], F32, kind="ExternalInput").ap()
    wvt = nc.dram_tensor("wvt", [D, D], F32, kind="ExternalInput").ap()
    woutt = nc.dram_tensor("woutt", [128, 8192], F32, kind="ExternalInput").ap()
    bandh = nc.dram_tensor("bandh", [H, 128, 3968], F32, kind="ExternalInput").ap()
    wgt = nc.dram_tensor("wgt", [D, FF], I32, kind="ExternalInput").ap()
    wut = nc.dram_tensor("wut", [D, FF], I32, kind="ExternalInput").ap()
    wdt = nc.dram_tensor("wdt", [FF, D], I32, kind="ExternalInput").ap()
    identi = nc.dram_tensor("identi", [128, 128], BF16, kind="ExternalInput").ap()
    onesqi = nc.dram_tensor("onesqi", [128, 128], BF16, kind="ExternalInput").ap()
    outt = nc.dram_tensor("outt", [SEQN, D, T], F32, kind="ExternalOutput").ap()
    tap_out = {}
    for tname, shape, dt in taps:
        tap_out[tname] = nc.dram_tensor("tap_" + tname, shape, dt,
                                        kind="ExternalOutput").ap()

    rec = Rec()
    O = rec.op
    with contextlib.ExitStack() as stk:
        E = stk.enter_context
        R12 = E(nc.sbuf_tensor("R12", [128, 16384], F32))
        R3 = E(nc.sbuf_tensor("R3", [128, 16384], BF16))
        R4 = E(nc.sbuf_tensor("R4", [128, 24832], BF16))
        R5 = E(nc.sbuf_tensor("R5", [128, 4096], F32))
        SCR = E(nc.sbuf_tensor("SCR", [128, 14336], BF16))
        Rsb = E(nc.sbuf_tensor("Rsb", [64, 1024], BF16))
        pvo = E(nc.sbuf_tensor("pvo", [64, 1024], BF16))
        rrow = E(nc.sbuf_tensor("rrow", [128, 1024], BF16))
        lkmrow = E(nc.sbuf_tensor("lkmrow", [1, 8], F32))
        rows_f = E(nc.sbuf_tensor("rows_f", [1, 1536], F32))
        rows_b = E(nc.sbuf_tensor("rows_b", [1, 2048], BF16))
        ident = E(nc.sbuf_tensor("ident", [128, 128], BF16))
        onesq = E(nc.sbuf_tensor("onesq", [128, 128], BF16))
        mz = E(nc.sbuf_tensor("mz", [128, 1024], BF16))
        cpool = E(nc.sbuf_tensor("cpool", [128, 8], F32))
        rot = E(nc.psum_tensor("rot", [128, 2048], F32))
        pv = E(nc.psum_tensor("pv", [128, 512], F32))
        rbc = E(nc.psum_tensor("rbc", [128, 512], F32))
        st1 = E(nc.psum_tensor("st1", [128, 512], F32))
        st2 = E(nc.psum_tensor("st2", [128, 512], F32))

        R12b = R12[:].bitcast(BF16)
        qf = R12b[:, 0:16384]
        kf = R12b[:, 16384:32768]
        x1 = R12[:]
        x1_3d = x1.rearrange("p (c n) -> p c n", c=NF, n=T)
        xhat = R3[:]
        attnout = R3[:]
        ybF = R3[:, 0:8192]
        vaug3 = R4[:, 0:16640].rearrange("p (b e) -> p b e", e=65)
        wslot = R4[:, 16640:24832]
        wv3 = wslot.rearrange("p (c n) -> p c n", c=NF, n=1024)
        wo3 = wslot.rearrange("p (c n) -> p c n", c=NF, n=1024)
        hbuf = R4[:, 0:24576]
        xin = R5[:]
        xh2 = R5[:].bitcast(BF16)
        cp = lambda j: cpool[:, j:j + 1]
        cp1 = lambda j: cpool[0:1, j:j + 1]

        WS = [0, 1024, 2048, 3072]
        DD = [4096, 7168]
        SQ = [10240, 10752]
        TS = [11264, 11776]
        XB = [12288, 12800]
        SS = [13312, 13824]

        def scr(c0, w):
            return SCR[:, c0:c0 + w]

        def ms(ap_, v):
            return lambda eng, a=ap_, vv=v: eng.memset(a, vv)
        for j, v in ((0, EXP_SHIFT), (1, QSCALE), (2, EPS), (3, -0.5), (4, NEG_LN127),
                     (5, -1.0), (6, 1e-37), (7, 0.5)):
            O("DVE", ms(cpool[:, j:j + 1], v), writes=("cp",))
        O("DSP", lambda eng: eng.dma_start(out=ident[:], in_=identi), writes=("ident",))
        O("DSP", lambda eng: eng.dma_start(out=onesq[:], in_=onesqi), writes=("onesq",))
        O("DVE", lambda eng: eng.memset(mz[:], 0.0), writes=("mz0", "mz1"))
        O("DVE", lambda eng: eng.memset(rrow[:], 0.0), writes=("rr0", "rr1"))
        O("DVE", lambda eng: eng.memset(SCR[:, 12288:14336], 0.0),
          writes=_scr_keys(12288, 14336))

        rotk = [f"rot{i}" for i in range(4)]

        def mm(out_ap, lhsT, rhs, start, stop, reads, writes):
            return O("PE", lambda eng, o=out_ap, l=lhsT, r=rhs, s1=start, s2=stop:
                     eng.matmul(o, l, r, start=s1, stop=s2),
                     reads=reads, writes=writes)

        def stats_rows(par, width, scale_j=None):
            m = rows_f[0:1, par * 512: par * 512 + width]
            var = rows_f[0:1, 1024:1024 + width]
            a = rows_b[0:1, par * 512: par * 512 + width]
            nb = rows_b[0:1, 1024 + par * 512: 1024 + par * 512 + width]
            mk = f"RF{par}"
            O("DVE", lambda eng: eng.tensor_scalar_mul(m, st1[0:1, 0:width], INV_D),
              reads=("st1",), writes=(mk,))
            O("DVE", lambda eng: eng.tensor_tensor(var, m, m, ALU.mult),
              reads=(mk,), writes=("RF2",))
            O("DVE", lambda eng: eng.scalar_tensor_tensor(
                var, st2[0:1, 0:width], INV_D, var, ALU.mult, ALU.subtract),
              reads=("st2", "RF2"), writes=("RF2",))
            O("ACT", lambda eng: eng.activation(var, var, AF.Ln, bias=cp1(2)),
              reads=("RF2", "cp"), writes=("RF2",))
            bias_arg = cp1(scale_j) if scale_j is not None else 0.0
            O("ACT", lambda eng, b=bias_arg: eng.activation(
                a, var, AF.Exp, bias=b, scale=cp1(3)),
              reads=("RF2", "cp"), writes=(f"ra{par}",))
            O("DVE", lambda eng: eng.tensor_scalar_mul(nb, m, -1.0),
              reads=(mk,), writes=(f"rb{par}",))
            return a, nb

        def bcast(dst_ap, row_ap, rowkey, dstkey, width):
            mm(dst_ap, onesq[0:1, 0:128], row_ap, True, True,
               ("onesq", rowkey), (dstkey,))

        def stats_pair(nkeys, st1_ap, st2_ap, xb_ap, sq_ap, xbk, sqk, first, last):
            mm(st1_ap, onesq[:, 0:1], xb_ap, first, last, ("onesq", xbk), ("st1",))
            mm(st2_ap, onesq[:, 0:1], sq_ap, first, last, ("onesq", sqk), ("st2",))

        def stage_dma(dst_ap, src_ap, wkeys):
            return O("DGP", lambda eng, d=dst_ap, s=src_ap:
                     eng.dma_start(out=d, in_=s), writes=wkeys)

        rotc = [0]
        carryF = [None]

        def next_rot():
            i = rotc[0] % 4
            rotc[0] += 1
            return rot[:, i * 512:(i + 1) * 512], rotk[i]

        out_dma_tokens = []

        for s in range(SEQN):
            stage_dma(wv3, wvt.rearrange("(c p) n -> p c n", p=128), ("w40", "w41"))
            O("DVE", lambda eng: eng.memset(vaug3[:, :, 64:65], 1.0), writes=("vaug",))

            for c8 in range(8):
                p5 = c8 % 2
                xin_h = xin[:, p5 * 2048:(p5 + 1) * 2048]
                xin_3 = xin_h.rearrange("p (c n) -> p c n", c=NF, n=256)
                src = xt[s, :, c8 * 256:(c8 + 1) * 256].rearrange(
                    "(c p) n -> p c n", p=128)
                O("DSP", lambda eng, d=xin_3, ss=src: eng.dma_start(out=d, in_=ss),
                  writes=tuple(f"XH{p5 * 4 + j}" for j in range(4)))
                for f in range(NF):
                    fp = f % 2
                    xb = scr(XB[fp], 256)
                    sq = scr(SQ[fp], 256)
                    xf = xin_3[:, f, :]
                    O("ACT", lambda eng, a=xb, b=xf: eng.activation(a, b, AF.Copy),
                      reads=(f"XH{p5 * 4 + f // 2}",), writes=(_scr_keys(XB[fp], XB[fp] + 512)[0],))
                    O("ACT", lambda eng, a=sq, b=xf: eng.activation(a, b, AF.Square),
                      reads=(f"XH{p5 * 4 + f // 2}",), writes=(_scr_keys(SQ[fp], SQ[fp] + 512)[0],))
                    stats_pair(None, st1[0:1, 0:256], st2[0:1, 0:256], xb, sq,
                               _scr_keys(XB[fp], XB[fp] + 512)[0], _scr_keys(SQ[fp], SQ[fp] + 512)[0], f == 0, f == NF - 1)
                par = c8 % 2
                a, nb = stats_rows(par, 256)
                if carryF[0] is not None:
                    carryF[0]()
                    carryF[0] = None
                A_ps, Ak = pv[:, 0:256], "pv"
                B_ps, Bk = rbc[:, 0:256], "rbc"
                bcast(A_ps, a, f"ra{par}", Ak, 256)
                bcast(B_ps, nb, f"rb{par}", Bk, 256)
                for f in range(NF):
                    tsl = scr(TS[f % 2], 256)
                    xf = xin_3[:, f, :]
                    xo = xhat[:, f * T + c8 * 256: f * T + (c8 + 1) * 256]
                    O("DVE", lambda eng, o=tsl, i0=xf, i1=B_ps:
                      eng.tensor_tensor(o, i0, i1, ALU.add),
                      reads=(f"XH{p5 * 4 + f // 2}", Bk), writes=(_scr_keys(TS[f % 2], TS[f % 2] + 512)[0],))
                    O("DVE", lambda eng, o=xo, i0=tsl, i1=A_ps:
                      eng.tensor_tensor(o, i0, i1, ALU.mult),
                      reads=(_scr_keys(TS[f % 2], TS[f % 2] + 512)[0], Ak), writes=R3K)

            if stop_after == "A0":
                break
            PT4 = [4096, 5120]
            NSQ = [2048, 2048]
            ES = [0, 1024]
            LQ0 = 6144
            MROW0 = 8192

            def norm_steps(h, nsq=2048):
                hc, rb = h // 2, (h % 2) * 64
                par = h % 2
                lq = SCR[0:1, LQ0:LQ0 + 2048]
                pbm = MROW0 + par * 2048
                lk = SCR[0:1, pbm:pbm + 2048]
                mrow_full = SCR[0:1, pbm:pbm + 2048]
                lkm = lkmrow[0:1, 0:1]
                halfkm = lkmrow[0:1, 1:2]
                lqk = _scr_keys(LQ0, LQ0 + 2048)
                lkk = _scr_keys(pbm, pbm + 2048)
                steps = []
                for src_f, srck, outrow, outk in (
                        (qf, f"QK{hc}", lq, lqk),
                        (kf, f"QK{8 + hc}", lk, lkk)):
                    sq_ap = SCR[rb:rb + 64, nsq:nsq + 2048]
                    blk = src_f[:, hc * T: (hc + 1) * T][rb:rb + 64, :]
                    steps.append(lambda sq_ap=sq_ap, blk=blk, srck=srck: O(
                        "DVE", lambda eng, o=sq_ap, i=blk: eng.tensor_tensor(
                            o, i, i, ALU.mult),
                        reads=(srck,), writes=_scr_keys(nsq, nsq + 2048)))
                    for c in range(4):
                        def step(c=c, outrow=outrow, outk=outk, rb=rb):
                            mm(st1[0:1, 0:512], onesq[rb:rb + 64, 0:1],
                               SCR[rb:rb + 64, nsq + c * 512: nsq + (c + 1) * 512],
                               True, True,
                               ("onesq",) + _scr_keys(nsq, nsq + 2048),
                               ("st1",))
                            O("ACT", lambda eng, o=outrow[0:1, c * 512:(c + 1) * 512]:
                              eng.activation(o, st1[0:1, 0:512], AF.Ln),
                              reads=("st1",), writes=outk)
                        steps.append(step)
                steps.append(lambda: O(
                    "DVE", lambda eng, o=lq, i=lq: eng.tensor_scalar_mul(o, i, 0.5),
                    reads=lqk, writes=lqk))
                steps.append(lambda: O(
                    "DVE", lambda eng, o=lk, i=lk: eng.tensor_scalar_mul(o, i, 0.5),
                    reads=lkk, writes=lkk))
                steps.append(lambda: O(
                    "DVE", lambda eng, o=halfkm: eng.reduce_max(
                        o, lk, axis=mybir.AxisListType.X),
                    reads=lkk, writes=("hkm",)))
                steps.append(lambda: O(
                    "ACT", lambda eng, o=mrow_full, i=lq: eng.activation(
                        o, i, AF.Exp, bias=halfkm),
                    reads=lqk + ("hkm",), writes=lkk))
                steps.append(lambda: O(
                    "DVE", lambda eng, o=mrow_full: eng.tensor_scalar_mul(o, o, -1.0),
                    reads=lkk, writes=lkk))
                return mrow_full, lkk, steps

            for kc in range(16):
                for nh in range(2):
                    rr, rk = next_rot()
                    for f in range(NF):
                        mm(rr, xhat[:, f * T + kc * 128: f * T + (kc + 1) * 128],
                           wv3[:, f, nh * 512:(nh + 1) * 512],
                           f == 0, f == NF - 1,
                           R3K + ("w40", "w41"), (rk,))
                    b0 = kc * 16 + 8 * nh
                    dst = vaug3[:, b0:b0 + 8, 0:64]
                    O("ACT", lambda eng, d=dst, ss=rr: eng.activation(d, ss, AF.Copy),
                      reads=(rk,), writes=("vaug",))

            def qk_stage(i):
                dst = scr(WS[i % 4], 1024).rearrange("p (c n) -> p c n", n=128)
                src = wqkt[:, i * 128:(i + 1) * 128].rearrange("(c p) n -> p c n", p=128)
                stage_dma(dst, src, (f"S{WS[i % 4] // 512}", f"S{WS[i % 4] // 512 + 1}"))

            qk_stage(0)
            qk_stage(1)
            h0p = [None, None, []]
            for i in range(16):
                wsk = (f"S{WS[i % 4] // 512}", f"S{WS[i % 4] // 512 + 1}")
                w3 = scr(WS[i % 4], 1024).rearrange("p (c n) -> p c n", n=128)
                for n in range(4):
                    rr, rk = next_rot()
                    for f in range(NF):
                        mm(rr, w3[:, f, :], xhat[:, f * T + n * 512: f * T + (n + 1) * 512],
                           f == 0, f == NF - 1, R3K + wsk, (rk,))
                    if i < 8:
                        dst = qf[:, i * T + n * 512: i * T + (n + 1) * 512]
                        O("ACT", lambda eng, d=dst, ss=rr: eng.activation(
                            d, ss, AF.Copy, scale=cp(1)),
                          reads=(rk, "cp"), writes=(f"QK{i}",))
                    else:
                        dst = kf[:, (i - 8) * T + n * 512: (i - 8) * T + (n + 1) * 512]
                        O("DVE", lambda eng, d=dst, ss=rr: eng.tensor_copy(d, ss),
                          reads=(rk,), writes=(f"QK{i}",))
                    if h0p[2]:
                        h0p[2].pop(0)()
                if i == 8:
                    h0p = list(norm_steps(0, nsq=4096))
                if i + 2 < 16:
                    qk_stage(i + 2)

            if stop_after == "A1":
                break
            O("DVE", lambda eng: eng.memset(SCR[:, 12288:14336], 0.0),
              writes=_scr_keys(12288, 14336))
            stage_dma(R4[:, 16640:16640 + 3968], bandh[0], ("band0", "w40", "w41"))
            O("ACT", lambda eng, o=R4[:, 16640:16640 + 3968]: eng.activation(
                o, o, AF.Exp), reads=("band0",), writes=("band0",))

            def emit_norms(h):
                mrow_full, lkk, steps = norm_steps(h)
                for f in steps:
                    f()
                return mrow_full, lkk

            iters = [(h, qt) for h in range(H) for qt in range(4)]
            for f in h0p[2]:
                f()
            h0p[2] = []
            mrow_of = {}
            mrow_of[0] = (h0p[0], h0p[1])

            def emit_qzmz(t):
                h, qt = iters[t]
                hc, rb = h // 2, (h % 2) * 64
                qsl = 12288 + ((2 if rb else 0) + t % 2) * 512
                qz = SCR[:, qsl:qsl + 512]
                qzk = _scr_keys(qsl, qsl + 512)
                q_src = qf[:, hc * T + qt * 512: hc * T + (qt + 1) * 512][rb:rb + 64, :]
                O("DVE", lambda eng, o=qz[rb:rb + 64, :], i=q_src: eng.tensor_copy(o, i),
                  reads=(f"QK{hc}",), writes=qzk)
                mrow_full, mrk = mrow_of[h]
                mzsl = (t % 2) * 512
                mzk = (f"mz{t % 2}",)
                mr = mrow_full[0:1, qt * 512:(qt + 1) * 512]
                O("DVE", lambda eng, o=mz[0:1, mzsl:mzsl + 512], i=mr: eng.tensor_copy(
                    o, i),
                  reads=mrk, writes=mzk)

            hookq = []

            def emit_P2(t, mid=None):
                h, qt = iters[t]
                hc, rb = h // 2, (h % 2) * 64
                bsl = h % 2
                bkey = f"band{bsl}"
                bnd = R4[:, 16640 + bsl * 3968: 16640 + bsl * 3968 + 3968]
                qsl = 12288 + ((2 if rb else 0) + t % 2) * 512
                qz = SCR[:, qsl:qsl + 512]
                qzk = _scr_keys(qsl, qsl + 512)
                mzsl = (t % 2) * 512
                mzk = (f"mz{t % 2}",)
                mzv = mz[:, mzsl:mzsl + 512]

                PSLOT = [0, 1024, 4096, 5120]

                def pv_pair(g):
                    base = PSLOT[g % 4]
                    for jj in range(2):
                        kc = 2 * g + jj
                        pt_ap = scr(base + jj * 512, 512)
                        mm(pv[0:65, :], vaug3[:, kc * 16 + h, :], pt_ap,
                           kc == 0, kc == 15,
                           ("vaug", f"S{(base + jj * 512) // 512}"), ("pv",))

                for g in range(8):
                    if g == 4 and mid is not None:
                        mid()
                    if hookq:
                        hookq.pop(0)()
                    for jj in range(2):
                        kc = 2 * g + jj
                        rsl = 2 * (g % 2) + jj
                        rr = rot[:, rsl * 512:(rsl + 1) * 512]
                        k_l = kf[:, hc * T + kc * 128: hc * T + (kc + 1) * 128]
                        mm(rr, k_l, qz, True, False, (f"QK{8 + hc}",) + qzk, (rotk[rsl],))
                        mm(rr, onesq[:], mzv, False, True,
                           ("onesq",) + mzk, (rotk[rsl],))
                    if g == 5 and t + 1 < NIT:
                        emit_qzmz(t + 1)
                    base = PSLOT[g % 4]
                    es_ap = scr(base, 1024)
                    esk = _scr_keys(base, base + 1024)
                    rpair = rot[:, (g % 2) * 1024:(g % 2 + 1) * 1024]
                    O("ACT", lambda eng, d=es_ap, ss=rpair: eng.activation(
                        d, ss, AF.Exp),
                      reads=(rotk[2 * (g % 2)], rotk[2 * (g % 2) + 1]), writes=esk)
                    for jj in range(2):
                        kc = 2 * g + jj
                        x0 = 1920 + qt * 512 - kc * 128
                        pt_ap = scr(base + jj * 512, 512)
                        ptk1 = (f"S{(base + jj * 512) // 512}",)
                        ev = bnd[:, x0:x0 + 512]
                        O("DVE", lambda eng, d=pt_ap, i1=ev:
                          eng.tensor_tensor(d, d, i1, ALU.mult),
                          reads=esk + (bkey,), writes=ptk1)
                    if g > 1:
                        pv_pair(g - 2)
                pv_pair(6)
                pv_pair(7)

            def emit_fin_a(t):
                h, qt = iters[t]
                par = t % 2
                rf = rows_f[0:1, 0:512]
                O("ACT", lambda eng: eng.activation(rf, pv[64:65, :], AF.Ln),
                  reads=("pv",), writes=("RF0",))
                O("ACT", lambda eng, o=rrow[0:1, par * 512:(par + 1) * 512]:
                  eng.activation(o, rf, AF.Exp, scale=cp1(5)),
                  reads=("RF0", "cp"), writes=(f"rr{par}",))
                O("DVE", lambda eng, o=pvo[:, par * 512:(par + 1) * 512]:
                  eng.tensor_copy(o, pv[0:64, :]),
                  reads=("pv",), writes=(f"pvo{par}",))

            def emit_fin_b(t):
                h, qt = iters[t]
                hc = h // 2
                par = t % 2
                mm(rbc[0:64, :], onesq[:, 0:64], rrow[:, par * 512:(par + 1) * 512],
                   True, True, ("onesq", f"rr{par}"), ("rbc",))
                Rs = Rsb[:, par * 512:(par + 1) * 512]
                O("DVE", lambda eng, o=Rs: eng.tensor_copy(o, rbc[0:64, :]),
                  reads=("rbc",), writes=(f"Rs{par}",))
                cols = hc * T + qt * 512
                pvs = pvo[:, par * 512:(par + 1) * 512]
                if h % 2 == 0:
                    dst = attnout[0:64, cols:cols + 512]
                    O("DVE", lambda eng, d=dst, i0=pvs, i1=Rs: eng.tensor_tensor(
                        d, i0, i1, ALU.mult),
                      reads=(f"pvo{par}", f"Rs{par}"), writes=(f"R3:{h}",))
                else:
                    O("DVE", lambda eng, d=pvs, i1=Rs: eng.tensor_tensor(
                        d, d, i1, ALU.mult),
                      reads=(f"pvo{par}", f"Rs{par}"), writes=(f"pvo{par}",))
                    dst = attnout[64:128, cols:cols + 512]
                    O("DSP", lambda eng, d=dst, ss=pvs: eng.dma_start(
                        out=d, in_=ss),
                      reads=(f"pvo{par}",), writes=(f"R3:{h}",))

            NIT = len(iters)
            emit_qzmz(0)
            for t in range(NIT):
                h, qt = iters[t]
                if t == NIT - 4 and H == 16:
                    stage_dma(wslot[:, 0:3968], woutt[:, 0:3968],
                              ("band0", "w40"))
                    srcv0 = xt[s, :, 0:512].rearrange("(c p) n -> p c n", p=128)
                    O("DSP", lambda eng, d=xin.rearrange(
                        "p (c n) -> p c n", c=NF, n=512), ss=srcv0:
                      eng.dma_start(out=d, in_=ss),
                      writes=tuple(f"XH{j}" for j in range(8)))
                if qt == 1 and h + 1 < H:
                    nsl = (h + 1) % 2
                    wk = ("band1", "w41") if nsl == 1 else ("band0", "w40")
                    bsl_ap = R4[:, 16640 + nsl * 3968:
                                16640 + nsl * 3968 + 3968]
                    stage_dma(bsl_ap, bandh[h + 1], wk)
                    mrow_full, lkk, steps = norm_steps(h + 1)
                    bsteps = []
                    for c4 in range(4):
                        bv = R4[:, 16640 + nsl * 3968 + c4 * 992:
                                16640 + nsl * 3968 + (c4 + 1) * 992]
                        bsteps.append(lambda bv=bv, wk=wk: O(
                            "ACT", lambda eng, o=bv: eng.activation(o, o, AF.Exp),
                            reads=wk[0:1], writes=wk[0:1]))
                    mrow_of[h + 1] = (mrow_full, lkk)
                    assert not hookq
                    hookq.extend(bsteps + steps)
                emit_P2(t, mid=(lambda tt=t - 1: emit_fin_b(tt)) if t >= 1 else None)
                emit_fin_a(t)
            assert not hookq
            emit_fin_b(NIT - 1)

            if stop_after == "B":
                break
            stage_dma(wslot[:, 3968:8192], woutt[:, 3968:8192],
                      ("band1", "w41"))
            xin_3 = xin.rearrange("p (c n) -> p c n", c=NF, n=512)

            def emit_xin_dma(qt):
                srcv = xt[s, :, qt * 512:(qt + 1) * 512].rearrange(
                    "(c p) n -> p c n", p=128)
                O("DSP", lambda eng, d=xin_3, ss=srcv: eng.dma_start(out=d, in_=ss),
                  writes=tuple(f"XH{j}" for j in range(8)))

            def make_post2(qt, yb2, yb2keys):
                par = qt % 2
                a, nb = stats_rows(par, 512)

                def post2():
                    A_ps, Ak = pv[:], "pv"
                    B_ps, Bk = rbc[:], "rbc"
                    bcast(A_ps, a, f"ra{par}", Ak, 512)
                    bcast(B_ps, nb, f"rb{par}", Bk, 512)
                    for fo in range(NF):
                        tk = _scr_keys(TS[fo % 2], TS[fo % 2] + 512)[0]
                        tsl = scr(TS[fo % 2], 512)
                        yfo = yb2[:, fo, :]
                        xi = xin_3[:, fo, :]
                        xo = x1_3d[:, fo, qt * 512:(qt + 1) * 512]
                        x1k = f"QK{2 * fo + qt // 2}"
                        O("DVE", lambda eng, o=tsl, i0=yfo, i1=B_ps:
                          eng.tensor_tensor(o, i0, i1, ALU.add),
                          reads=yb2keys + (Bk,), writes=(tk,))
                        O("DVE", lambda eng, o=tsl, i1=A_ps:
                          eng.tensor_tensor(o, o, i1, ALU.mult),
                          reads=(tk, Ak), writes=(tk,))
                        O("DVE", lambda eng, o=xo, i0=tsl, i1=xi:
                          eng.tensor_tensor(o, i0, i1, ALU.add),
                          reads=(tk, f"XH{fo}"),
                          writes=(x1k,))
                        if qt < 3:
                            xsrc = xt[s, fo * 128:(fo + 1) * 128,
                                      (qt + 1) * 512:(qt + 2) * 512]
                            O("DSP", lambda eng, d=xi, ss=xsrc:
                              eng.dma_start(out=d, in_=ss),
                              writes=(f"XH{fo}",))
                return post2

            post2p = [None]
            for qt in range(4):
                yb2base = (qt % 2) * 4096
                yb2 = scr(yb2base, 4096).rearrange("p (c n) -> p c n", c=NF, n=512)
                yb2keys = _scr_keys(yb2base, yb2base + 4096)
                pend2 = [None]
                for fo in range(NF):
                    rr, rk = next_rot()
                    for kc in range(NF):
                        woff = _wo_tile(fo, kc)
                        wk2 = ("band0", "w40") if woff < 3968 else ("band1", "w41")
                        mm(rr, wslot[:, woff:woff + 128],
                           attnout[:, kc * T + qt * 512: kc * T + (qt + 1) * 512],
                           kc == 0, kc == NF - 1,
                           wk2 + (f"R3:{2 * kc}", f"R3:{2 * kc + 1}"), (rk,))
                    yfo = yb2[:, fo, :]
                    O("ACT", lambda eng, d=yfo, ss=rr: eng.activation(d, ss, AF.Copy),
                      reads=(rk,), writes=yb2keys)
                    sqk = _scr_keys(SQ[fo % 2], SQ[fo % 2] + 512)[0]
                    sq = scr(SQ[fo % 2], 512)
                    O("ACT", lambda eng, a=sq, b=yfo: eng.activation(
                        a, b, AF.Square),
                      reads=yb2keys, writes=(sqk,))
                    if fo == 4 and post2p[0] is not None:
                        post2p[0]()
                        post2p[0] = None
                    if pend2[0] is not None:
                        pend2[0]()
                    def st_emit(fo=fo, yfo=yfo, sq=sq, sqk=sqk):
                        stats_pair(None, st1[0:1, :], st2[0:1, :], yfo, sq,
                                   yb2keys[fo], sqk, fo == 0, fo == NF - 1)
                    pend2[0] = st_emit
                pend2[0]()
                post2p[0] = make_post2(qt, yb2, yb2keys)

            if stop_after == "B2":
                if post2p[0] is not None:
                    post2p[0]()
                break

            def emit_ln2(tc):
                for nh2 in range(2):
                    coff = tc * 1024 + nh2 * 512
                    for f in range(NF):
                        fp = f % 2
                        xbk = _scr_keys(XB[fp], XB[fp] + 512)[0]
                        sqk = _scr_keys(SQ[fp], SQ[fp] + 512)[0]
                        xb = scr(XB[fp], 512)
                        sq = scr(SQ[fp], 512)
                        xf = x1_3d[:, f, coff:coff + 512]
                        x1k = f"QK{2 * f + tc}"
                        O("ACT", lambda eng, a=xb, b=xf: eng.activation(
                            a, b, AF.Copy), reads=(x1k,), writes=(xbk,))
                        O("ACT", lambda eng, a=sq, b=xf: eng.activation(
                            a, b, AF.Square), reads=(x1k,), writes=(sqk,))
                        stats_pair(None, st1[0:1, :], st2[0:1, :], xb, sq,
                                   xbk, sqk, f == 0, f == NF - 1)
                    par = nh2
                    a, nb = stats_rows(par, 512, scale_j=4)
                    A_ps, Ak = pv[:], "pv"
                    B_ps, Bk = rbc[:], "rbc"
                    bcast(A_ps, a, f"ra{par}", Ak, 512)
                    bcast(B_ps, nb, f"rb{par}", Bk, 512)
                    for f in range(NF):
                        xf = x1_3d[:, f, coff:coff + 512]
                        x1k = f"QK{2 * f + tc}"
                        xo = xh2[:, f * 1024 + nh2 * 512: f * 1024 + (nh2 + 1) * 512]
                        xok = f"XH{f}"
                        O("DVE", lambda eng, o=xo, i0=xf, i1=B_ps:
                          eng.tensor_tensor(o, i0, i1, ALU.add),
                          reads=(x1k, Bk), writes=(xok,))
                        O("DVE", lambda eng, o=xo, i1=A_ps:
                          eng.tensor_tensor(o, o, i1, ALU.mult),
                          reads=(xok, Ak), writes=(xok,))

            def gu_stage(g):
                for w_src, base in ((wgt, 0), (wut, 2)):
                    sl = WS[base + g % 2]
                    dst = scr(sl, 1024).rearrange("p (c n) -> p c n", n=128)
                    srcv = w_src[:, g * 128:(g + 1) * 128].rearrange(
                        "(c p) n -> p c n", p=128)
                    stage_dma(dst, srcv, _scr_keys(sl, sl + 1024))

            def wd_stage(fo):
                sl = DD[fo % 2]
                dst = scr(sl, 3072).rearrange("p (c n) -> p c n", n=128)
                srcv = wdt[:, fo * 128:(fo + 1) * 128].rearrange(
                    "(c p) n -> p c n", p=128)
                stage_dma(dst, srcv, _scr_keys(sl, sl + 3072))

            ybF3 = ybF.rearrange("p (c n) -> p c n", c=NF, n=1024)

            def emit_rowsF():
                for nh in range(2):
                    par = nh
                    m = rows_f[0:1, par * 512: par * 512 + 512]
                    var = rows_f[0:1, 1024:1536]
                    a = rrow[0:1, par * 512:(par + 1) * 512]
                    nb = Rsb[0:1, par * 512:(par + 1) * 512]
                    s1ap = st1[0:1, 0:512] if nh == 0 else st2[0:1, 0:512]
                    s2ap = pv[0:1, 0:512] if nh == 0 else rbc[0:1, 0:512]
                    s1k = "st1" if nh == 0 else "st2"
                    s2k = "pv" if nh == 0 else "rbc"
                    mk = f"RF{par}"
                    O("DVE", lambda eng, o=m, i=s1ap: eng.tensor_scalar_mul(
                        o, i, INV_D), reads=(s1k,), writes=(mk,))
                    O("DVE", lambda eng, o=var, i=m: eng.tensor_tensor(
                        o, i, i, ALU.mult), reads=(mk,), writes=("RF2",))
                    O("DVE", lambda eng, o=var, i0=s2ap, i1=var:
                      eng.scalar_tensor_tensor(o, i0, INV_D, i1, ALU.mult,
                                               ALU.subtract),
                      reads=(s2k, "RF2"), writes=("RF2",))
                    O("ACT", lambda eng, o=var: eng.activation(
                        o, o, AF.Ln, bias=cp1(2)), reads=("RF2", "cp"),
                      writes=("RF2",))
                    O("ACT", lambda eng, o=a, i=var: eng.activation(
                        o, i, AF.Exp, scale=cp1(3)), reads=("RF2", "cp"),
                      writes=(f"rr{par}",))
                    O("DVE", lambda eng, o=nb, i=m: eng.tensor_scalar_mul(
                        o, i, -1.0), reads=(mk,), writes=(f"Rs{par}",))

            def make_postF(tc, s=s):
                def postF():
                    for nh in range(2):
                        par = nh
                        a = rrow[0:1, par * 512:(par + 1) * 512]
                        nb = Rsb[0:1, par * 512:(par + 1) * 512]
                        if tc == 0:
                            A_ps, Ak = st1[:, 0:512], "st1"
                            B_ps, Bk = st2[:, 0:512], "st2"
                        else:
                            A_ps, Ak = pv[:], "pv"
                            B_ps, Bk = rbc[:], "rbc"
                        bcast(A_ps, a, f"rr{par}", Ak, 512)
                        bcast(B_ps, nb, f"Rs{par}", Bk, 512)
                        for fo in range(NF):
                            tk = _scr_keys(TS[fo % 2], TS[fo % 2] + 512)[0]
                            tsl = scr(TS[fo % 2], 512)
                            yfo = ybF3[:, fo, nh * 512:(nh + 1) * 512]
                            xo = x1_3d[:, fo,
                                       tc * 1024 + nh * 512: tc * 1024 + (nh + 1) * 512]
                            x1k = f"QK{2 * fo + tc}"
                            O("DVE", lambda eng, o=tsl, i0=yfo, i1=B_ps:
                              eng.tensor_tensor(o, i0, i1, ALU.add),
                              reads=R3K + (Bk,), writes=(tk,))
                            O("DVE", lambda eng, o=tsl, i1=A_ps:
                              eng.tensor_tensor(o, o, i1, ALU.mult),
                              reads=(tk, Ak), writes=(tk,))
                            O("DVE", lambda eng, o=xo, i0=tsl:
                              eng.tensor_tensor(o, i0, o, ALU.add),
                              reads=(tk, x1k), writes=(x1k,))
                        coff = tc * 1024 + nh * 512
                        srcv = x1_3d[:, :, coff:coff + 512]
                        dstv = outt[s].rearrange("(c p) n -> p c n",
                                                 p=128)[:, :, coff:coff + 512]
                        tok = O("DSP", lambda eng, d=dstv, ss=srcv: eng.dma_start(
                            out=d, in_=ss),
                            reads=tuple(f"QK{2 * fo + tc}" for fo in range(NF)))
                        out_dma_tokens.append(tok)
                return postF

            postFp = [None]
            if post2p[0] is not None:
                post2p[0]()
                post2p[0] = None
            for tc in range(2):
                if tc == 0:
                    emit_ln2(0)
                gu_stage(0)
                gu_stage(1)
                for g in range(24):
                    wg3 = scr(WS[g % 2], 1024).rearrange("p (c n) -> p c n", n=128)
                    wu3 = scr(WS[2 + g % 2], 1024).rearrange("p (c n) -> p c n", n=128)
                    wgk = _scr_keys(WS[g % 2], WS[g % 2] + 1024)
                    wuk = _scr_keys(WS[2 + g % 2], WS[2 + g % 2] + 1024)
                    if g == 4 and postFp[0] is not None:
                        postFp[0]()
                        postFp[0] = None
                    for nh in range(2):
                        rg, rgk = next_rot()
                        ru, ruk = next_rot()
                        for f in range(NF):
                            xr = xh2[:, f * 1024 + nh * 512: f * 1024 + (nh + 1) * 512]
                            xrk = (f"XH{f}",)
                            mm(rg, wg3[:, f, :], xr, f == 0, f == NF - 1,
                               wgk + xrk, (rgk,))
                        for f in range(NF):
                            xr = xh2[:, f * 1024 + nh * 512: f * 1024 + (nh + 1) * 512]
                            xrk = (f"XH{f}",)
                            mm(ru, wu3[:, f, :], xr, f == 0, f == NF - 1,
                               wuk + xrk, (ruk,))
                        ssk = _scr_keys(SS[nh], SS[nh] + 512)[0]
                        ssl = scr(SS[nh], 512)
                        O("ACT", lambda eng, d=ssl, ss=rg: eng.activation(
                            d, ss, AF.Silu),
                          reads=(rgk,), writes=(ssk,))
                        hdst = hbuf[:, g * 1024 + nh * 512: g * 1024 + (nh + 1) * 512]
                        O("DVE", lambda eng, d=hdst, i0=ssl, i1=ru:
                          eng.tensor_tensor(d, i0, i1, ALU.mult),
                          reads=(ssk, ruk), writes=R4K)
                    if g + 2 < 24:
                        gu_stage(g + 2)

                wd_stage(0)
                wd_stage(1)
                pend3 = [None]
                for fo in range(NF):
                    wd3 = scr(DD[fo % 2], 3072).rearrange("p (c n) -> p c n", n=128)
                    wdk = _scr_keys(DD[fo % 2], DD[fo % 2] + 3072)
                    for nh in range(2):
                        rr, rk = next_rot()
                        for kc in range(24):
                            hsrc = hbuf[:, kc * 1024 + nh * 512: kc * 1024 + (nh + 1) * 512]
                            mm(rr, wd3[:, kc, :], hsrc, kc == 0, kc == 23,
                               wdk + R4K, (rk,))
                        yfo = ybF3[:, fo, nh * 512:(nh + 1) * 512]
                        O("ACT", lambda eng, d=yfo, ss=rr: eng.activation(
                            d, ss, AF.Copy),
                          reads=(rk,), writes=R3K)
                        sqk = _scr_keys(SQ[nh], SQ[nh] + 512)[0]
                        sq = scr(SQ[nh], 512)
                        O("ACT", lambda eng, a=sq, b=yfo: eng.activation(
                            a, b, AF.Square),
                          reads=R3K, writes=(sqk,))
                        if pend3[0] is not None:
                            pend3[0]()
                        def st_emit(fo=fo, nh=nh, yfo=yfo, sq=sq, sqk=sqk):
                            s1ap = st1[0:1, :] if nh == 0 else st2[0:1, :]
                            s2ap = pv[0:1, 0:512] if nh == 0 else rbc[0:1, 0:512]
                            s1k = "st1" if nh == 0 else "st2"
                            s2k = "pv" if nh == 0 else "rbc"
                            mm(s1ap, onesq[:, 0:1], yfo, fo == 0, fo == NF - 1,
                               ("onesq",) + R3K, (s1k,))
                            mm(s2ap, onesq[:, 0:1], sq, fo == 0, fo == NF - 1,
                               ("onesq", sqk), (s2k,))
                        pend3[0] = st_emit
                    if fo + 2 < NF:
                        wd_stage(fo + 2)
                pend3[0]()
                emit_rowsF()
                if tc + 1 < 2:
                    emit_ln2(tc + 1)
                postFp[0] = make_postF(tc)
            carryF[0] = postFp[0]

        if carryF[0] is not None:
            carryF[0]()
            carryF[0] = None

        tapmap = {"xhat": (R3[:], BF16), "qf": (qf, BF16), "kf": (kf, BF16),
                  "vaug": (R4[:, 0:16640], BF16), "attnout": (R3[:], BF16),
                  "x1": (x1, F32), "xh2": (xh2, BF16), "hbuf": (hbuf, BF16), "scr": (SCR[:], BF16)}
        for tname in tap_out:
            if tname not in tapmap:
                continue
            src_ap, _dt = tapmap[tname]
            O("DSP", lambda eng, d=tap_out[tname], ss=src_ap: eng.dma_start(
                out=d, in_=ss),
              reads=tuple(f"QK{i}" for i in range(16)) + R3K + R4K + tuple(f"XH{j}" for j in range(8)))

        def fin(eng):
            return eng.engine_nop()
        O("GP", fin, reads=tuple(f"QK{i}" for i in range(16)))
        rec.ops[-1] = (rec.ops[-1][0], rec.ops[-1][1],
                       frozenset(out_dma_tokens), rec.ops[-1][3])

        _emit_program(nc, rec)
    build_nc.last_rec = rec
    return nc


_NC_CACHE = {}


def _get_nc(taps=()):
    key = tuple(taps)
    if key not in _NC_CACHE:
        _NC_CACHE[key] = build_nc(taps)
    return _NC_CACHE[key]


def _host_prep(L0_attn_x, L0_attn_w_qkv, L0_attn_w_out, L0_attn_rel_bias,
               L0_ffn_w_gate, L0_ffn_w_up, L0_ffn_w_down):
    x = np.asarray(L0_attn_x, dtype=np.float32)
    w_qkv = np.asarray(L0_attn_w_qkv, dtype=np.float32)
    w_out = np.asarray(L0_attn_w_out, dtype=np.float32)
    rel_bias = np.asarray(L0_attn_rel_bias, dtype=np.float32)
    wg = np.asarray(L0_ffn_w_gate, dtype=np.int32)
    wu = np.asarray(L0_ffn_w_up, dtype=np.int32)
    wd = np.asarray(L0_ffn_w_down, dtype=np.int32)

    hh, dd = np.meshgrid(np.arange(H), np.arange(DH), indexing="ij")
    perm_q = (48 * dd + 0 * 16 + hh).reshape(-1)
    perm_k = (48 * dd + 1 * 16 + hh).reshape(-1)
    perm_v = (48 * dd + 2 * 16 + hh).reshape(-1)
    wqkt = np.ascontiguousarray(w_qkv[np.concatenate([perm_q, perm_k])].T)
    wvt = np.ascontiguousarray(w_qkv[perm_v].T)
    wo_t = w_out.T
    woutt = np.empty((128, 8192), np.float32)
    for fo in range(8):
        for kc in range(8):
            t0 = _wo_tile(fo, kc)
            woutt[:, t0:t0 + 128] = wo_t[kc * 128:(kc + 1) * 128,
                                         fo * 128:(fo + 1) * 128]
    wgt = np.ascontiguousarray(wg.T)
    wut = np.ascontiguousarray(wu.T)
    wdt = np.ascontiguousarray(wd.T)
    nrel = rel_bias.shape[0]
    ext = rel_bias[np.minimum(np.arange(2 * T - 1), nrel - 1)]
    idx = np.arange(128)[:, None] - np.arange(3968)[None, :] + 3967
    bandh = np.ascontiguousarray(ext[idx, :].transpose(2, 0, 1))
    identb = np.eye(128, dtype=np.float32).astype(ml_dtypes.bfloat16)
    onesqb = np.ones((128, 128), np.float32).astype(ml_dtypes.bfloat16)
    xt_all = np.ascontiguousarray(x.transpose(0, 2, 1))

    shared = dict(wqkt=wqkt, wvt=wvt, woutt=woutt, bandh=bandh,
                  wgt=wgt, wut=wut, wdt=wdt,
                  identi=identb, onesqi=onesqb)
    in_maps = []
    for c in range(NCORES):
        m = dict(shared)
        m["xt"] = np.ascontiguousarray(xt_all[c * SEQN:(c + 1) * SEQN])
        in_maps.append(m)
    return in_maps


def kernel(**inputs):
    in_maps = _host_prep(**inputs)
    nc = _get_nc()
    res = run_bass_kernel_spmd(nc, in_maps, core_ids=list(range(NCORES)))
    outs = []
    for c in range(NCORES):
        o = res.results[c]["outt"]
        outs.append(np.ascontiguousarray(np.transpose(o, (0, 2, 1))))
    return np.concatenate(outs, axis=0)


# version 38
# speedup vs baseline: 1.0011x; 1.0011x over previous
import contextlib
import numpy as np
import ml_dtypes
import concourse.bass as bass
import concourse.mybir as mybir
from concourse.bass_utils import run_bass_kernel_spmd

F32 = mybir.dt.float32
BF16 = mybir.dt.bfloat16
I32 = mybir.dt.int32
AF = mybir.ActivationFunctionType
ALU = mybir.AluOpType

NCORES = 8
SEQN = 2
T = 2048
D = 1024
H = 16
DH = 64
FF = 3072
NF = 8
INV_D = 1.0 / float(D)
EPS = 1e-5
EXP_SHIFT = -20.0
QSCALE = 0.125
NEG_LN127 = -float(np.log(127.0))

DMAQ = ("DSP", "DGP")
ENG_OF = {"PE": "tensor", "ACT": "scalar", "DVE": "vector",
          "DSP": "sync", "DGP": "gpsimd", "GP": "gpsimd"}


class Rec:

    def __init__(self):
        self.ops = []
        self.dbg = []
        self.cnt = {}
        self.last_w = {}
        self.readers = {}

    def op(self, q, emit, reads=(), writes=()):
        idx = self.cnt.get(q, 0)
        self.cnt[q] = idx + 1
        deps = set()
        for r in reads:
            w = self.last_w.get(r)
            if w is not None:
                deps.add(w)
        for r in writes:
            w = self.last_w.get(r)
            if w is not None:
                deps.add(w)
            rd = self.readers.get(r)
            if rd:
                deps.update(rd.items())
                self.readers[r] = {}
        for r in reads:
            rd = self.readers.setdefault(r, {})
            if rd.get(q, -1) < idx:
                rd[q] = idx
        for r in writes:
            self.last_w[r] = (q, idx)
        deps = frozenset(d for d in deps if d[0] != q)
        self.ops.append((q, emit, deps, idx))
        self.dbg.append((q, idx, tuple(reads), tuple(writes), deps))
        return (q, idx)


def _emit_program(nc, rec):
    with contextlib.ExitStack() as stk:
        sems = {}
        for q in ("PE", "ACT", "DVE", "DSP", "DGP"):
            sems[q] = stk.enter_context(nc.semaphore("sem_" + q))
        marked = {}
        for q, emit, deps, idx in rec.ops:
            for (q2, i2) in deps:
                if q2 not in DMAQ:
                    marked.setdefault(q2, set()).add(i2)
        rank = {q: {i: r + 1 for r, i in enumerate(sorted(s))}
                for q, s in marked.items()}
        per_eng = {"tensor": [], "scalar": [], "vector": [], "sync": [], "gpsimd": []}
        for rec_op in rec.ops:
            per_eng[ENG_OF[rec_op[0]]].append(rec_op)

        def mk_body(ops_list):
            def body(eng):
                waited = {}
                for q, emit, deps, idx in ops_list:
                    for (q2, i2) in sorted(deps):
                        if q2 in DMAQ:
                            val = 16 * (i2 + 1)
                        else:
                            val = rank[q2][i2]
                        if waited.get(q2, 0) < val:
                            eng.wait_ge(sems[q2], val)
                            waited[q2] = val
                    inst = emit(eng)
                    if q in DMAQ:
                        inst.then_inc(sems[q], 16)
                    elif idx in marked.get(q, ()):
                        inst.then_inc(sems[q], 1)
            return body

        block = stk.enter_context(nc.Block())
        block.tensor(mk_body(per_eng["tensor"]))
        block.scalar(mk_body(per_eng["scalar"]))
        block.vector(mk_body(per_eng["vector"]))
        block.sync(mk_body(per_eng["sync"]))
        block.gpsimd(mk_body(per_eng["gpsimd"]))


def _scr_keys(c0, c1):
    return tuple(f"S{i}" for i in range(c0 // 512, (c1 + 511) // 512))


R3K = tuple(f"R3:{h}" for h in range(H))
R4K = ("vaug", "w40", "w41")



def _wo_tile(fo, kc):
    if fo < 3:
        t = fo * 8 + kc
    elif fo == 3:
        t = 24 + kc if kc < 7 else 63
    else:
        t = 31 + (fo - 4) * 8 + kc
    return t * 128


def build_nc(taps=(), stop_after=None):
    nc = bass.Bass("TRN2", target_bir_lowering=False)
    xt = nc.dram_tensor("xt", [SEQN, D, T], F32, kind="ExternalInput").ap()
    wqkt = nc.dram_tensor("wqkt", [D, 2048], F32, kind="ExternalInput").ap()
    wvt = nc.dram_tensor("wvt", [D, D], F32, kind="ExternalInput").ap()
    woutt = nc.dram_tensor("woutt", [128, 8192], F32, kind="ExternalInput").ap()
    bandh = nc.dram_tensor("bandh", [H, 128, 3968], F32, kind="ExternalInput").ap()
    wgt = nc.dram_tensor("wgt", [D, FF], I32, kind="ExternalInput").ap()
    wut = nc.dram_tensor("wut", [D, FF], I32, kind="ExternalInput").ap()
    wdt = nc.dram_tensor("wdt", [FF, D], I32, kind="ExternalInput").ap()
    identi = nc.dram_tensor("identi", [128, 128], BF16, kind="ExternalInput").ap()
    onesqi = nc.dram_tensor("onesqi", [128, 128], BF16, kind="ExternalInput").ap()
    outt = nc.dram_tensor("outt", [SEQN, D, T], F32, kind="ExternalOutput").ap()
    tap_out = {}
    for tname, shape, dt in taps:
        tap_out[tname] = nc.dram_tensor("tap_" + tname, shape, dt,
                                        kind="ExternalOutput").ap()

    rec = Rec()
    O = rec.op
    with contextlib.ExitStack() as stk:
        E = stk.enter_context
        R12 = E(nc.sbuf_tensor("R12", [128, 16384], F32))
        R3 = E(nc.sbuf_tensor("R3", [128, 16384], BF16))
        R4 = E(nc.sbuf_tensor("R4", [128, 24832], BF16))
        R5 = E(nc.sbuf_tensor("R5", [128, 4096], F32))
        SCR = E(nc.sbuf_tensor("SCR", [128, 14336], BF16))
        Rsb = E(nc.sbuf_tensor("Rsb", [64, 1024], BF16))
        pvo = E(nc.sbuf_tensor("pvo", [64, 1024], BF16))
        rrow = E(nc.sbuf_tensor("rrow", [128, 1024], BF16))
        lkmrow = E(nc.sbuf_tensor("lkmrow", [1, 8], F32))
        rows_f = E(nc.sbuf_tensor("rows_f", [1, 1536], F32))
        rows_b = E(nc.sbuf_tensor("rows_b", [1, 2048], BF16))
        ident = E(nc.sbuf_tensor("ident", [128, 128], BF16))
        onesq = E(nc.sbuf_tensor("onesq", [128, 128], BF16))
        mz = E(nc.sbuf_tensor("mz", [128, 1024], BF16))
        cpool = E(nc.sbuf_tensor("cpool", [128, 8], F32))
        rot = E(nc.psum_tensor("rot", [128, 2048], F32))
        pv = E(nc.psum_tensor("pv", [128, 512], F32))
        rbc = E(nc.psum_tensor("rbc", [128, 512], F32))
        st1 = E(nc.psum_tensor("st1", [128, 512], F32))
        st2 = E(nc.psum_tensor("st2", [128, 512], F32))

        R12b = R12[:].bitcast(BF16)
        qf = R12b[:, 0:16384]
        kf = R12b[:, 16384:32768]
        x1 = R12[:]
        x1_3d = x1.rearrange("p (c n) -> p c n", c=NF, n=T)
        xhat = R3[:]
        attnout = R3[:]
        ybF = R3[:, 0:8192]
        vaug3 = R4[:, 0:16640].rearrange("p (b e) -> p b e", e=65)
        wslot = R4[:, 16640:24832]
        wv3 = wslot.rearrange("p (c n) -> p c n", c=NF, n=1024)
        wo3 = wslot.rearrange("p (c n) -> p c n", c=NF, n=1024)
        hbuf = R4[:, 0:24576]
        xin = R5[:]
        xh2 = R5[:].bitcast(BF16)
        cp = lambda j: cpool[:, j:j + 1]
        cp1 = lambda j: cpool[0:1, j:j + 1]

        WS = [0, 1024, 2048, 3072]
        DD = [4096, 7168]
        SQ = [10240, 10752]
        TS = [11264, 11776]
        XB = [12288, 12800]
        SS = [13312, 13824]

        def scr(c0, w):
            return SCR[:, c0:c0 + w]

        def ms(ap_, v):
            return lambda eng, a=ap_, vv=v: eng.memset(a, vv)
        for j, v in ((0, EXP_SHIFT), (1, QSCALE), (2, EPS), (3, -0.5), (4, NEG_LN127),
                     (5, -1.0), (6, 1e-37), (7, 0.5)):
            O("DVE", ms(cpool[:, j:j + 1], v), writes=("cp",))
        O("DSP", lambda eng: eng.dma_start(out=ident[:], in_=identi), writes=("ident",))
        O("DSP", lambda eng: eng.dma_start(out=onesq[:], in_=onesqi), writes=("onesq",))
        O("DVE", lambda eng: eng.memset(mz[:], 0.0), writes=("mz0", "mz1"))
        O("DVE", lambda eng: eng.memset(rrow[:], 0.0), writes=("rr0", "rr1"))
        O("DVE", lambda eng: eng.memset(SCR[:, 12288:14336], 0.0),
          writes=_scr_keys(12288, 14336))

        rotk = [f"rot{i}" for i in range(4)]

        def mm(out_ap, lhsT, rhs, start, stop, reads, writes):
            return O("PE", lambda eng, o=out_ap, l=lhsT, r=rhs, s1=start, s2=stop:
                     eng.matmul(o, l, r, start=s1, stop=s2),
                     reads=reads, writes=writes)

        def stats_rows(par, width, scale_j=None):
            m = rows_f[0:1, par * 512: par * 512 + width]
            var = rows_f[0:1, 1024:1024 + width]
            a = rows_b[0:1, par * 512: par * 512 + width]
            nb = rows_b[0:1, 1024 + par * 512: 1024 + par * 512 + width]
            mk = f"RF{par}"
            O("DVE", lambda eng: eng.tensor_scalar_mul(m, st1[0:1, 0:width], INV_D),
              reads=("st1",), writes=(mk,))
            O("DVE", lambda eng: eng.tensor_tensor(var, m, m, ALU.mult),
              reads=(mk,), writes=("RF2",))
            O("DVE", lambda eng: eng.scalar_tensor_tensor(
                var, st2[0:1, 0:width], INV_D, var, ALU.mult, ALU.subtract),
              reads=("st2", "RF2"), writes=("RF2",))
            O("ACT", lambda eng: eng.activation(var, var, AF.Ln, bias=cp1(2)),
              reads=("RF2", "cp"), writes=("RF2",))
            bias_arg = cp1(scale_j) if scale_j is not None else 0.0
            O("ACT", lambda eng, b=bias_arg: eng.activation(
                a, var, AF.Exp, bias=b, scale=cp1(3)),
              reads=("RF2", "cp"), writes=(f"ra{par}",))
            O("DVE", lambda eng: eng.tensor_scalar_mul(nb, m, -1.0),
              reads=(mk,), writes=(f"rb{par}",))
            return a, nb

        def bcast(dst_ap, row_ap, rowkey, dstkey, width):
            mm(dst_ap, onesq[0:1, 0:128], row_ap, True, True,
               ("onesq", rowkey), (dstkey,))

        def stats_pair(nkeys, st1_ap, st2_ap, xb_ap, sq_ap, xbk, sqk, first, last):
            mm(st1_ap, onesq[:, 0:1], xb_ap, first, last, ("onesq", xbk), ("st1",))
            mm(st2_ap, onesq[:, 0:1], sq_ap, first, last, ("onesq", sqk), ("st2",))

        def stage_dma(dst_ap, src_ap, wkeys):
            return O("DGP", lambda eng, d=dst_ap, s=src_ap:
                     eng.dma_start(out=d, in_=s), writes=wkeys)

        rotc = [0]
        carryF = [None]

        def next_rot():
            i = rotc[0] % 4
            rotc[0] += 1
            return rot[:, i * 512:(i + 1) * 512], rotk[i]

        out_dma_tokens = []

        for s in range(SEQN):
            stage_dma(wv3, wvt.rearrange("(c p) n -> p c n", p=128), ("w40", "w41"))
            O("DVE", lambda eng: eng.memset(vaug3[:, :, 64:65], 1.0), writes=("vaug",))

            for c8 in range(8):
                p5 = c8 % 2
                xin_h = xin[:, p5 * 2048:(p5 + 1) * 2048]
                xin_3 = xin_h.rearrange("p (c n) -> p c n", c=NF, n=256)
                src = xt[s, :, c8 * 256:(c8 + 1) * 256].rearrange(
                    "(c p) n -> p c n", p=128)
                O("DSP", lambda eng, d=xin_3, ss=src: eng.dma_start(out=d, in_=ss),
                  writes=tuple(f"XH{p5 * 4 + j}" for j in range(4)))
                for f in range(NF):
                    fp = f % 2
                    xb = scr(XB[fp], 256)
                    sq = scr(SQ[fp], 256)
                    xf = xin_3[:, f, :]
                    O("ACT", lambda eng, a=xb, b=xf: eng.activation(a, b, AF.Copy),
                      reads=(f"XH{p5 * 4 + f // 2}",), writes=(_scr_keys(XB[fp], XB[fp] + 512)[0],))
                    O("ACT", lambda eng, a=sq, b=xf: eng.activation(a, b, AF.Square),
                      reads=(f"XH{p5 * 4 + f // 2}",), writes=(_scr_keys(SQ[fp], SQ[fp] + 512)[0],))
                    stats_pair(None, st1[0:1, 0:256], st2[0:1, 0:256], xb, sq,
                               _scr_keys(XB[fp], XB[fp] + 512)[0], _scr_keys(SQ[fp], SQ[fp] + 512)[0], f == 0, f == NF - 1)
                par = c8 % 2
                a, nb = stats_rows(par, 256)
                if carryF[0] is not None:
                    carryF[0]()
                    carryF[0] = None
                if c8 % 2 == 0:
                    A_ps, Ak = rot[:, 0:256], rotk[0]
                    B_ps, Bk = rot[:, 512:768], rotk[1]
                else:
                    A_ps, Ak = pv[:, 0:256], "pv"
                    B_ps, Bk = rbc[:, 0:256], "rbc"
                bcast(A_ps, a, f"ra{par}", Ak, 256)
                bcast(B_ps, nb, f"rb{par}", Bk, 256)
                for f in range(NF):
                    tsl = scr(TS[f % 2], 256)
                    xf = xin_3[:, f, :]
                    xo = xhat[:, f * T + c8 * 256: f * T + (c8 + 1) * 256]
                    O("DVE", lambda eng, o=tsl, i0=xf, i1=B_ps:
                      eng.tensor_tensor(o, i0, i1, ALU.add),
                      reads=(f"XH{p5 * 4 + f // 2}", Bk), writes=(_scr_keys(TS[f % 2], TS[f % 2] + 512)[0],))
                    O("DVE", lambda eng, o=xo, i0=tsl, i1=A_ps:
                      eng.tensor_tensor(o, i0, i1, ALU.mult),
                      reads=(_scr_keys(TS[f % 2], TS[f % 2] + 512)[0], Ak), writes=R3K)

            if stop_after == "A0":
                break
            PT4 = [4096, 5120]
            NSQ = [2048, 2048]
            ES = [0, 1024]
            LQ0 = 6144
            MROW0 = 8192

            def norm_steps(h, nsq=2048):
                hc, rb = h // 2, (h % 2) * 64
                par = h % 2
                lq = SCR[0:1, LQ0:LQ0 + 2048]
                pbm = MROW0 + par * 2048
                lk = SCR[0:1, pbm:pbm + 2048]
                mrow_full = SCR[0:1, pbm:pbm + 2048]
                lkm = lkmrow[0:1, 0:1]
                halfkm = lkmrow[0:1, 1:2]
                lqk = _scr_keys(LQ0, LQ0 + 2048)
                lkk = _scr_keys(pbm, pbm + 2048)
                steps = []
                for src_f, srck, outrow, outk in (
                        (qf, f"QK{hc}", lq, lqk),
                        (kf, f"QK{8 + hc}", lk, lkk)):
                    sq_ap = SCR[rb:rb + 64, nsq:nsq + 2048]
                    blk = src_f[:, hc * T: (hc + 1) * T][rb:rb + 64, :]
                    steps.append(lambda sq_ap=sq_ap, blk=blk, srck=srck: O(
                        "DVE", lambda eng, o=sq_ap, i=blk: eng.tensor_tensor(
                            o, i, i, ALU.mult),
                        reads=(srck,), writes=_scr_keys(nsq, nsq + 2048)))
                    for c in range(4):
                        def step(c=c, outrow=outrow, outk=outk, rb=rb):
                            mm(st1[0:1, 0:512], onesq[rb:rb + 64, 0:1],
                               SCR[rb:rb + 64, nsq + c * 512: nsq + (c + 1) * 512],
                               True, True,
                               ("onesq",) + _scr_keys(nsq, nsq + 2048),
                               ("st1",))
                            O("ACT", lambda eng, o=outrow[0:1, c * 512:(c + 1) * 512]:
                              eng.activation(o, st1[0:1, 0:512], AF.Ln),
                              reads=("st1",), writes=outk)
                        steps.append(step)
                steps.append(lambda: O(
                    "DVE", lambda eng, o=lq, i=lq: eng.tensor_scalar_mul(o, i, 0.5),
                    reads=lqk, writes=lqk))
                steps.append(lambda: O(
                    "DVE", lambda eng, o=lk, i=lk: eng.tensor_scalar_mul(o, i, 0.5),
                    reads=lkk, writes=lkk))
                steps.append(lambda: O(
                    "DVE", lambda eng, o=halfkm: eng.reduce_max(
                        o, lk, axis=mybir.AxisListType.X),
                    reads=lkk, writes=("hkm",)))
                steps.append(lambda: O(
                    "ACT", lambda eng, o=mrow_full, i=lq: eng.activation(
                        o, i, AF.Exp, bias=halfkm),
                    reads=lqk + ("hkm",), writes=lkk))
                steps.append(lambda: O(
                    "DVE", lambda eng, o=mrow_full: eng.tensor_scalar_mul(o, o, -1.0),
                    reads=lkk, writes=lkk))
                return mrow_full, lkk, steps

            for kc in range(16):
                for nh in range(2):
                    rr, rk = next_rot()
                    for f in range(NF):
                        mm(rr, xhat[:, f * T + kc * 128: f * T + (kc + 1) * 128],
                           wv3[:, f, nh * 512:(nh + 1) * 512],
                           f == 0, f == NF - 1,
                           R3K + ("w40", "w41"), (rk,))
                    b0 = kc * 16 + 8 * nh
                    dst = vaug3[:, b0:b0 + 8, 0:64]
                    O("ACT", lambda eng, d=dst, ss=rr: eng.activation(d, ss, AF.Copy),
                      reads=(rk,), writes=("vaug",))

            def qk_stage(i):
                dst = scr(WS[i % 4], 1024).rearrange("p (c n) -> p c n", n=128)
                src = wqkt[:, i * 128:(i + 1) * 128].rearrange("(c p) n -> p c n", p=128)
                stage_dma(dst, src, (f"S{WS[i % 4] // 512}", f"S{WS[i % 4] // 512 + 1}"))

            qk_stage(0)
            qk_stage(1)
            h0p = [None, None, []]
            for i in range(16):
                wsk = (f"S{WS[i % 4] // 512}", f"S{WS[i % 4] // 512 + 1}")
                w3 = scr(WS[i % 4], 1024).rearrange("p (c n) -> p c n", n=128)
                for n in range(4):
                    rr, rk = next_rot()
                    for f in range(NF):
                        mm(rr, w3[:, f, :], xhat[:, f * T + n * 512: f * T + (n + 1) * 512],
                           f == 0, f == NF - 1, R3K + wsk, (rk,))
                    if i < 8:
                        dst = qf[:, i * T + n * 512: i * T + (n + 1) * 512]
                        O("ACT", lambda eng, d=dst, ss=rr: eng.activation(
                            d, ss, AF.Copy, scale=cp(1)),
                          reads=(rk, "cp"), writes=(f"QK{i}",))
                    else:
                        dst = kf[:, (i - 8) * T + n * 512: (i - 8) * T + (n + 1) * 512]
                        O("DVE", lambda eng, d=dst, ss=rr: eng.tensor_copy(d, ss),
                          reads=(rk,), writes=(f"QK{i}",))
                    if h0p[2]:
                        h0p[2].pop(0)()
                if i == 8:
                    h0p = list(norm_steps(0, nsq=4096))
                if i + 2 < 16:
                    qk_stage(i + 2)

            if stop_after == "A1":
                break
            O("DVE", lambda eng: eng.memset(SCR[:, 12288:14336], 0.0),
              writes=_scr_keys(12288, 14336))
            stage_dma(R4[:, 16640:16640 + 3968], bandh[0], ("band0", "w40", "w41"))
            O("ACT", lambda eng, o=R4[:, 16640:16640 + 3968]: eng.activation(
                o, o, AF.Exp), reads=("band0",), writes=("band0",))

            def emit_norms(h):
                mrow_full, lkk, steps = norm_steps(h)
                for f in steps:
                    f()
                return mrow_full, lkk

            iters = [(h, qt) for h in range(H) for qt in range(4)]
            for f in h0p[2]:
                f()
            h0p[2] = []
            mrow_of = {}
            mrow_of[0] = (h0p[0], h0p[1])

            def emit_qzmz(t):
                h, qt = iters[t]
                hc, rb = h // 2, (h % 2) * 64
                qsl = 12288 + ((2 if rb else 0) + t % 2) * 512
                qz = SCR[:, qsl:qsl + 512]
                qzk = _scr_keys(qsl, qsl + 512)
                q_src = qf[:, hc * T + qt * 512: hc * T + (qt + 1) * 512][rb:rb + 64, :]
                O("DVE", lambda eng, o=qz[rb:rb + 64, :], i=q_src: eng.tensor_copy(o, i),
                  reads=(f"QK{hc}",), writes=qzk)
                mrow_full, mrk = mrow_of[h]
                mzsl = (t % 2) * 512
                mzk = (f"mz{t % 2}",)
                mr = mrow_full[0:1, qt * 512:(qt + 1) * 512]
                O("DVE", lambda eng, o=mz[0:1, mzsl:mzsl + 512], i=mr: eng.tensor_copy(
                    o, i),
                  reads=mrk, writes=mzk)

            hookq = []

            def emit_P2(t, mid=None):
                h, qt = iters[t]
                hc, rb = h // 2, (h % 2) * 64
                bsl = h % 2
                bkey = f"band{bsl}"
                bnd = R4[:, 16640 + bsl * 3968: 16640 + bsl * 3968 + 3968]
                qsl = 12288 + ((2 if rb else 0) + t % 2) * 512
                qz = SCR[:, qsl:qsl + 512]
                qzk = _scr_keys(qsl, qsl + 512)
                mzsl = (t % 2) * 512
                mzk = (f"mz{t % 2}",)
                mzv = mz[:, mzsl:mzsl + 512]

                PSLOT = [0, 1024, 4096, 5120]

                def pv_pair(g):
                    base = PSLOT[g % 4]
                    for jj in range(2):
                        kc = 2 * g + jj
                        pt_ap = scr(base + jj * 512, 512)
                        mm(pv[0:65, :], vaug3[:, kc * 16 + h, :], pt_ap,
                           kc == 0, kc == 15,
                           ("vaug", f"S{(base + jj * 512) // 512}"), ("pv",))

                for g in range(8):
                    if g == 4 and mid is not None:
                        mid()
                    if hookq:
                        hookq.pop(0)()
                    for jj in range(2):
                        kc = 2 * g + jj
                        rsl = 2 * (g % 2) + jj
                        rr = rot[:, rsl * 512:(rsl + 1) * 512]
                        k_l = kf[:, hc * T + kc * 128: hc * T + (kc + 1) * 128]
                        mm(rr, k_l, qz, True, False, (f"QK{8 + hc}",) + qzk, (rotk[rsl],))
                        mm(rr, onesq[:], mzv, False, True,
                           ("onesq",) + mzk, (rotk[rsl],))
                    if g == 5 and t + 1 < NIT:
                        emit_qzmz(t + 1)
                    base = PSLOT[g % 4]
                    es_ap = scr(base, 1024)
                    esk = _scr_keys(base, base + 1024)
                    rpair = rot[:, (g % 2) * 1024:(g % 2 + 1) * 1024]
                    O("ACT", lambda eng, d=es_ap, ss=rpair: eng.activation(
                        d, ss, AF.Exp),
                      reads=(rotk[2 * (g % 2)], rotk[2 * (g % 2) + 1]), writes=esk)
                    for jj in range(2):
                        kc = 2 * g + jj
                        x0 = 1920 + qt * 512 - kc * 128
                        pt_ap = scr(base + jj * 512, 512)
                        ptk1 = (f"S{(base + jj * 512) // 512}",)
                        ev = bnd[:, x0:x0 + 512]
                        O("DVE", lambda eng, d=pt_ap, i1=ev:
                          eng.tensor_tensor(d, d, i1, ALU.mult),
                          reads=esk + (bkey,), writes=ptk1)
                    if g > 1:
                        pv_pair(g - 2)
                pv_pair(6)
                pv_pair(7)

            def emit_fin_a(t):
                h, qt = iters[t]
                par = t % 2
                rf = rows_f[0:1, 0:512]
                O("ACT", lambda eng: eng.activation(rf, pv[64:65, :], AF.Ln),
                  reads=("pv",), writes=("RF0",))
                O("ACT", lambda eng, o=rrow[0:1, par * 512:(par + 1) * 512]:
                  eng.activation(o, rf, AF.Exp, scale=cp1(5)),
                  reads=("RF0", "cp"), writes=(f"rr{par}",))
                O("DVE", lambda eng, o=pvo[:, par * 512:(par + 1) * 512]:
                  eng.tensor_copy(o, pv[0:64, :]),
                  reads=("pv",), writes=(f"pvo{par}",))

            def emit_fin_b(t):
                h, qt = iters[t]
                hc = h // 2
                par = t % 2
                mm(rbc[0:64, :], onesq[:, 0:64], rrow[:, par * 512:(par + 1) * 512],
                   True, True, ("onesq", f"rr{par}"), ("rbc",))
                Rs = Rsb[:, par * 512:(par + 1) * 512]
                O("DVE", lambda eng, o=Rs: eng.tensor_copy(o, rbc[0:64, :]),
                  reads=("rbc",), writes=(f"Rs{par}",))
                cols = hc * T + qt * 512
                pvs = pvo[:, par * 512:(par + 1) * 512]
                if h % 2 == 0:
                    dst = attnout[0:64, cols:cols + 512]
                    O("DVE", lambda eng, d=dst, i0=pvs, i1=Rs: eng.tensor_tensor(
                        d, i0, i1, ALU.mult),
                      reads=(f"pvo{par}", f"Rs{par}"), writes=(f"R3:{h}",))
                else:
                    O("DVE", lambda eng, d=pvs, i1=Rs: eng.tensor_tensor(
                        d, d, i1, ALU.mult),
                      reads=(f"pvo{par}", f"Rs{par}"), writes=(f"pvo{par}",))
                    dst = attnout[64:128, cols:cols + 512]
                    O("DSP", lambda eng, d=dst, ss=pvs: eng.dma_start(
                        out=d, in_=ss),
                      reads=(f"pvo{par}",), writes=(f"R3:{h}",))

            NIT = len(iters)
            emit_qzmz(0)
            for t in range(NIT):
                h, qt = iters[t]
                if t == NIT - 4 and H == 16:
                    stage_dma(wslot[:, 0:3968], woutt[:, 0:3968],
                              ("band0", "w40"))
                    srcv0 = xt[s, :, 0:512].rearrange("(c p) n -> p c n", p=128)
                    O("DSP", lambda eng, d=xin.rearrange(
                        "p (c n) -> p c n", c=NF, n=512), ss=srcv0:
                      eng.dma_start(out=d, in_=ss),
                      writes=tuple(f"XH{j}" for j in range(8)))
                if qt == 1 and h + 1 < H:
                    nsl = (h + 1) % 2
                    wk = ("band1", "w41") if nsl == 1 else ("band0", "w40")
                    bsl_ap = R4[:, 16640 + nsl * 3968:
                                16640 + nsl * 3968 + 3968]
                    stage_dma(bsl_ap, bandh[h + 1], wk)
                    mrow_full, lkk, steps = norm_steps(h + 1)
                    bsteps = []
                    for c4 in range(4):
                        bv = R4[:, 16640 + nsl * 3968 + c4 * 992:
                                16640 + nsl * 3968 + (c4 + 1) * 992]
                        bsteps.append(lambda bv=bv, wk=wk: O(
                            "ACT", lambda eng, o=bv: eng.activation(o, o, AF.Exp),
                            reads=wk[0:1], writes=wk[0:1]))
                    mrow_of[h + 1] = (mrow_full, lkk)
                    assert not hookq
                    hookq.extend(bsteps + steps)
                emit_P2(t, mid=(lambda tt=t - 1: emit_fin_b(tt)) if t >= 1 else None)
                emit_fin_a(t)
            assert not hookq
            emit_fin_b(NIT - 1)

            if stop_after == "B":
                break
            stage_dma(wslot[:, 3968:8192], woutt[:, 3968:8192],
                      ("band1", "w41"))
            xin_3 = xin.rearrange("p (c n) -> p c n", c=NF, n=512)

            def emit_xin_dma(qt):
                srcv = xt[s, :, qt * 512:(qt + 1) * 512].rearrange(
                    "(c p) n -> p c n", p=128)
                O("DSP", lambda eng, d=xin_3, ss=srcv: eng.dma_start(out=d, in_=ss),
                  writes=tuple(f"XH{j}" for j in range(8)))

            def make_post2(qt, yb2, yb2keys):
                par = qt % 2
                a, nb = stats_rows(par, 512)

                def post2():
                    A_ps, Ak = pv[:], "pv"
                    B_ps, Bk = rbc[:], "rbc"
                    bcast(A_ps, a, f"ra{par}", Ak, 512)
                    bcast(B_ps, nb, f"rb{par}", Bk, 512)
                    for fo in range(NF):
                        tk = _scr_keys(TS[fo % 2], TS[fo % 2] + 512)[0]
                        tsl = scr(TS[fo % 2], 512)
                        yfo = yb2[:, fo, :]
                        xi = xin_3[:, fo, :]
                        xo = x1_3d[:, fo, qt * 512:(qt + 1) * 512]
                        x1k = f"QK{2 * fo + qt // 2}"
                        O("DVE", lambda eng, o=tsl, i0=yfo, i1=B_ps:
                          eng.tensor_tensor(o, i0, i1, ALU.add),
                          reads=yb2keys + (Bk,), writes=(tk,))
                        O("DVE", lambda eng, o=tsl, i1=A_ps:
                          eng.tensor_tensor(o, o, i1, ALU.mult),
                          reads=(tk, Ak), writes=(tk,))
                        O("DVE", lambda eng, o=xo, i0=tsl, i1=xi:
                          eng.tensor_tensor(o, i0, i1, ALU.add),
                          reads=(tk, f"XH{fo}"),
                          writes=(x1k,))
                        if qt < 3:
                            xsrc = xt[s, fo * 128:(fo + 1) * 128,
                                      (qt + 1) * 512:(qt + 2) * 512]
                            O("DSP", lambda eng, d=xi, ss=xsrc:
                              eng.dma_start(out=d, in_=ss),
                              writes=(f"XH{fo}",))
                return post2

            post2p = [None]
            for qt in range(4):
                yb2base = (qt % 2) * 4096
                yb2 = scr(yb2base, 4096).rearrange("p (c n) -> p c n", c=NF, n=512)
                yb2keys = _scr_keys(yb2base, yb2base + 4096)
                pend2 = [None]
                for fo in range(NF):
                    rr, rk = next_rot()
                    for kc in range(NF):
                        woff = _wo_tile(fo, kc)
                        wk2 = ("band0", "w40") if woff < 3968 else ("band1", "w41")
                        mm(rr, wslot[:, woff:woff + 128],
                           attnout[:, kc * T + qt * 512: kc * T + (qt + 1) * 512],
                           kc == 0, kc == NF - 1,
                           wk2 + (f"R3:{2 * kc}", f"R3:{2 * kc + 1}"), (rk,))
                    yfo = yb2[:, fo, :]
                    O("ACT", lambda eng, d=yfo, ss=rr: eng.activation(d, ss, AF.Copy),
                      reads=(rk,), writes=yb2keys)
                    sqk = _scr_keys(SQ[fo % 2], SQ[fo % 2] + 512)[0]
                    sq = scr(SQ[fo % 2], 512)
                    O("ACT", lambda eng, a=sq, b=yfo: eng.activation(
                        a, b, AF.Square),
                      reads=yb2keys, writes=(sqk,))
                    if fo == 4 and post2p[0] is not None:
                        post2p[0]()
                        post2p[0] = None
                    if pend2[0] is not None:
                        pend2[0]()
                    def st_emit(fo=fo, yfo=yfo, sq=sq, sqk=sqk):
                        stats_pair(None, st1[0:1, :], st2[0:1, :], yfo, sq,
                                   yb2keys[fo], sqk, fo == 0, fo == NF - 1)
                    pend2[0] = st_emit
                pend2[0]()
                post2p[0] = make_post2(qt, yb2, yb2keys)

            if stop_after == "B2":
                if post2p[0] is not None:
                    post2p[0]()
                break

            def emit_ln2(tc):
                for nh2 in range(2):
                    coff = tc * 1024 + nh2 * 512
                    for f in range(NF):
                        fp = f % 2
                        xbk = _scr_keys(XB[fp], XB[fp] + 512)[0]
                        sqk = _scr_keys(SQ[fp], SQ[fp] + 512)[0]
                        xb = scr(XB[fp], 512)
                        sq = scr(SQ[fp], 512)
                        xf = x1_3d[:, f, coff:coff + 512]
                        x1k = f"QK{2 * f + tc}"
                        O("ACT", lambda eng, a=xb, b=xf: eng.activation(
                            a, b, AF.Copy), reads=(x1k,), writes=(xbk,))
                        O("ACT", lambda eng, a=sq, b=xf: eng.activation(
                            a, b, AF.Square), reads=(x1k,), writes=(sqk,))
                        stats_pair(None, st1[0:1, :], st2[0:1, :], xb, sq,
                                   xbk, sqk, f == 0, f == NF - 1)
                    par = nh2
                    a, nb = stats_rows(par, 512, scale_j=4)
                    A_ps, Ak = pv[:], "pv"
                    B_ps, Bk = rbc[:], "rbc"
                    bcast(A_ps, a, f"ra{par}", Ak, 512)
                    bcast(B_ps, nb, f"rb{par}", Bk, 512)
                    for f in range(NF):
                        xf = x1_3d[:, f, coff:coff + 512]
                        x1k = f"QK{2 * f + tc}"
                        xo = xh2[:, f * 1024 + nh2 * 512: f * 1024 + (nh2 + 1) * 512]
                        xok = f"XH{f}"
                        O("DVE", lambda eng, o=xo, i0=xf, i1=B_ps:
                          eng.tensor_tensor(o, i0, i1, ALU.add),
                          reads=(x1k, Bk), writes=(xok,))
                        O("DVE", lambda eng, o=xo, i1=A_ps:
                          eng.tensor_tensor(o, o, i1, ALU.mult),
                          reads=(xok, Ak), writes=(xok,))

            def gu_stage(g):
                for w_src, base in ((wgt, 0), (wut, 2)):
                    sl = WS[base + g % 2]
                    dst = scr(sl, 1024).rearrange("p (c n) -> p c n", n=128)
                    srcv = w_src[:, g * 128:(g + 1) * 128].rearrange(
                        "(c p) n -> p c n", p=128)
                    stage_dma(dst, srcv, _scr_keys(sl, sl + 1024))

            def wd_stage(fo):
                sl = DD[fo % 2]
                dst = scr(sl, 3072).rearrange("p (c n) -> p c n", n=128)
                srcv = wdt[:, fo * 128:(fo + 1) * 128].rearrange(
                    "(c p) n -> p c n", p=128)
                stage_dma(dst, srcv, _scr_keys(sl, sl + 3072))

            ybF3 = ybF.rearrange("p (c n) -> p c n", c=NF, n=1024)

            def emit_rowsF():
                for nh in range(2):
                    par = nh
                    m = rows_f[0:1, par * 512: par * 512 + 512]
                    var = rows_f[0:1, 1024:1536]
                    a = rrow[0:1, par * 512:(par + 1) * 512]
                    nb = Rsb[0:1, par * 512:(par + 1) * 512]
                    s1ap = st1[0:1, 0:512] if nh == 0 else st2[0:1, 0:512]
                    s2ap = pv[0:1, 0:512] if nh == 0 else rbc[0:1, 0:512]
                    s1k = "st1" if nh == 0 else "st2"
                    s2k = "pv" if nh == 0 else "rbc"
                    mk = f"RF{par}"
                    O("DVE", lambda eng, o=m, i=s1ap: eng.tensor_scalar_mul(
                        o, i, INV_D), reads=(s1k,), writes=(mk,))
                    O("DVE", lambda eng, o=var, i=m: eng.tensor_tensor(
                        o, i, i, ALU.mult), reads=(mk,), writes=("RF2",))
                    O("DVE", lambda eng, o=var, i0=s2ap, i1=var:
                      eng.scalar_tensor_tensor(o, i0, INV_D, i1, ALU.mult,
                                               ALU.subtract),
                      reads=(s2k, "RF2"), writes=("RF2",))
                    O("ACT", lambda eng, o=var: eng.activation(
                        o, o, AF.Ln, bias=cp1(2)), reads=("RF2", "cp"),
                      writes=("RF2",))
                    O("ACT", lambda eng, o=a, i=var: eng.activation(
                        o, i, AF.Exp, scale=cp1(3)), reads=("RF2", "cp"),
                      writes=(f"rr{par}",))
                    O("DVE", lambda eng, o=nb, i=m: eng.tensor_scalar_mul(
                        o, i, -1.0), reads=(mk,), writes=(f"Rs{par}",))

            def make_postF(tc, s=s):
                def postF():
                    for nh in range(2):
                        par = nh
                        a = rrow[0:1, par * 512:(par + 1) * 512]
                        nb = Rsb[0:1, par * 512:(par + 1) * 512]
                        if tc == 0:
                            A_ps, Ak = st1[:, 0:512], "st1"
                            B_ps, Bk = st2[:, 0:512], "st2"
                        else:
                            A_ps, Ak = pv[:], "pv"
                            B_ps, Bk = rbc[:], "rbc"
                        bcast(A_ps, a, f"rr{par}", Ak, 512)
                        bcast(B_ps, nb, f"Rs{par}", Bk, 512)
                        for fo in range(NF):
                            tk = _scr_keys(TS[fo % 2], TS[fo % 2] + 512)[0]
                            tsl = scr(TS[fo % 2], 512)
                            yfo = ybF3[:, fo, nh * 512:(nh + 1) * 512]
                            xo = x1_3d[:, fo,
                                       tc * 1024 + nh * 512: tc * 1024 + (nh + 1) * 512]
                            x1k = f"QK{2 * fo + tc}"
                            O("DVE", lambda eng, o=tsl, i0=yfo, i1=B_ps:
                              eng.tensor_tensor(o, i0, i1, ALU.add),
                              reads=R3K + (Bk,), writes=(tk,))
                            O("DVE", lambda eng, o=tsl, i1=A_ps:
                              eng.tensor_tensor(o, o, i1, ALU.mult),
                              reads=(tk, Ak), writes=(tk,))
                            O("DVE", lambda eng, o=xo, i0=tsl:
                              eng.tensor_tensor(o, i0, o, ALU.add),
                              reads=(tk, x1k), writes=(x1k,))
                        coff = tc * 1024 + nh * 512
                        srcv = x1_3d[:, :, coff:coff + 512]
                        dstv = outt[s].rearrange("(c p) n -> p c n",
                                                 p=128)[:, :, coff:coff + 512]
                        tok = O("DSP", lambda eng, d=dstv, ss=srcv: eng.dma_start(
                            out=d, in_=ss),
                            reads=tuple(f"QK{2 * fo + tc}" for fo in range(NF)))
                        out_dma_tokens.append(tok)
                return postF

            postFp = [None]
            if post2p[0] is not None:
                post2p[0]()
                post2p[0] = None
            for tc in range(2):
                if tc == 0:
                    emit_ln2(0)
                gu_stage(0)
                gu_stage(1)
                for g in range(24):
                    wg3 = scr(WS[g % 2], 1024).rearrange("p (c n) -> p c n", n=128)
                    wu3 = scr(WS[2 + g % 2], 1024).rearrange("p (c n) -> p c n", n=128)
                    wgk = _scr_keys(WS[g % 2], WS[g % 2] + 1024)
                    wuk = _scr_keys(WS[2 + g % 2], WS[2 + g % 2] + 1024)
                    if g == 4 and postFp[0] is not None:
                        postFp[0]()
                        postFp[0] = None
                    for nh in range(2):
                        rg, rgk = next_rot()
                        ru, ruk = next_rot()
                        for f in range(NF):
                            xr = xh2[:, f * 1024 + nh * 512: f * 1024 + (nh + 1) * 512]
                            xrk = (f"XH{f}",)
                            mm(rg, wg3[:, f, :], xr, f == 0, f == NF - 1,
                               wgk + xrk, (rgk,))
                        for f in range(NF):
                            xr = xh2[:, f * 1024 + nh * 512: f * 1024 + (nh + 1) * 512]
                            xrk = (f"XH{f}",)
                            mm(ru, wu3[:, f, :], xr, f == 0, f == NF - 1,
                               wuk + xrk, (ruk,))
                        ssk = _scr_keys(SS[nh], SS[nh] + 512)[0]
                        ssl = scr(SS[nh], 512)
                        O("ACT", lambda eng, d=ssl, ss=rg: eng.activation(
                            d, ss, AF.Silu),
                          reads=(rgk,), writes=(ssk,))
                        hdst = hbuf[:, g * 1024 + nh * 512: g * 1024 + (nh + 1) * 512]
                        O("DVE", lambda eng, d=hdst, i0=ssl, i1=ru:
                          eng.tensor_tensor(d, i0, i1, ALU.mult),
                          reads=(ssk, ruk), writes=R4K)
                    if g + 2 < 24:
                        gu_stage(g + 2)

                wd_stage(0)
                wd_stage(1)
                pend3 = [None]
                for fo in range(NF):
                    wd3 = scr(DD[fo % 2], 3072).rearrange("p (c n) -> p c n", n=128)
                    wdk = _scr_keys(DD[fo % 2], DD[fo % 2] + 3072)
                    for nh in range(2):
                        rr, rk = next_rot()
                        for kc in range(24):
                            hsrc = hbuf[:, kc * 1024 + nh * 512: kc * 1024 + (nh + 1) * 512]
                            mm(rr, wd3[:, kc, :], hsrc, kc == 0, kc == 23,
                               wdk + R4K, (rk,))
                        yfo = ybF3[:, fo, nh * 512:(nh + 1) * 512]
                        O("ACT", lambda eng, d=yfo, ss=rr: eng.activation(
                            d, ss, AF.Copy),
                          reads=(rk,), writes=R3K)
                        sqk = _scr_keys(SQ[nh], SQ[nh] + 512)[0]
                        sq = scr(SQ[nh], 512)
                        O("ACT", lambda eng, a=sq, b=yfo: eng.activation(
                            a, b, AF.Square),
                          reads=R3K, writes=(sqk,))
                        if pend3[0] is not None:
                            pend3[0]()
                        def st_emit(fo=fo, nh=nh, yfo=yfo, sq=sq, sqk=sqk):
                            s1ap = st1[0:1, :] if nh == 0 else st2[0:1, :]
                            s2ap = pv[0:1, 0:512] if nh == 0 else rbc[0:1, 0:512]
                            s1k = "st1" if nh == 0 else "st2"
                            s2k = "pv" if nh == 0 else "rbc"
                            mm(s1ap, onesq[:, 0:1], yfo, fo == 0, fo == NF - 1,
                               ("onesq",) + R3K, (s1k,))
                            mm(s2ap, onesq[:, 0:1], sq, fo == 0, fo == NF - 1,
                               ("onesq", sqk), (s2k,))
                        pend3[0] = st_emit
                    if fo + 2 < NF:
                        wd_stage(fo + 2)
                pend3[0]()
                emit_rowsF()
                if tc + 1 < 2:
                    emit_ln2(tc + 1)
                postFp[0] = make_postF(tc)
            carryF[0] = postFp[0]

        if carryF[0] is not None:
            carryF[0]()
            carryF[0] = None

        tapmap = {"xhat": (R3[:], BF16), "qf": (qf, BF16), "kf": (kf, BF16),
                  "vaug": (R4[:, 0:16640], BF16), "attnout": (R3[:], BF16),
                  "x1": (x1, F32), "xh2": (xh2, BF16), "hbuf": (hbuf, BF16), "scr": (SCR[:], BF16)}
        for tname in tap_out:
            if tname not in tapmap:
                continue
            src_ap, _dt = tapmap[tname]
            O("DSP", lambda eng, d=tap_out[tname], ss=src_ap: eng.dma_start(
                out=d, in_=ss),
              reads=tuple(f"QK{i}" for i in range(16)) + R3K + R4K + tuple(f"XH{j}" for j in range(8)))

        def fin(eng):
            return eng.engine_nop()
        O("GP", fin, reads=tuple(f"QK{i}" for i in range(16)))
        rec.ops[-1] = (rec.ops[-1][0], rec.ops[-1][1],
                       frozenset(out_dma_tokens), rec.ops[-1][3])

        _emit_program(nc, rec)
    build_nc.last_rec = rec
    return nc


_NC_CACHE = {}


def _get_nc(taps=()):
    key = tuple(taps)
    if key not in _NC_CACHE:
        _NC_CACHE[key] = build_nc(taps)
    return _NC_CACHE[key]


def _host_prep(L0_attn_x, L0_attn_w_qkv, L0_attn_w_out, L0_attn_rel_bias,
               L0_ffn_w_gate, L0_ffn_w_up, L0_ffn_w_down):
    x = np.asarray(L0_attn_x, dtype=np.float32)
    w_qkv = np.asarray(L0_attn_w_qkv, dtype=np.float32)
    w_out = np.asarray(L0_attn_w_out, dtype=np.float32)
    rel_bias = np.asarray(L0_attn_rel_bias, dtype=np.float32)
    wg = np.asarray(L0_ffn_w_gate, dtype=np.int32)
    wu = np.asarray(L0_ffn_w_up, dtype=np.int32)
    wd = np.asarray(L0_ffn_w_down, dtype=np.int32)

    hh, dd = np.meshgrid(np.arange(H), np.arange(DH), indexing="ij")
    perm_q = (48 * dd + 0 * 16 + hh).reshape(-1)
    perm_k = (48 * dd + 1 * 16 + hh).reshape(-1)
    perm_v = (48 * dd + 2 * 16 + hh).reshape(-1)
    wqkt = np.ascontiguousarray(w_qkv[np.concatenate([perm_q, perm_k])].T)
    wvt = np.ascontiguousarray(w_qkv[perm_v].T)
    wo_t = w_out.T
    woutt = np.empty((128, 8192), np.float32)
    for fo in range(8):
        for kc in range(8):
            t0 = _wo_tile(fo, kc)
            woutt[:, t0:t0 + 128] = wo_t[kc * 128:(kc + 1) * 128,
                                         fo * 128:(fo + 1) * 128]
    wgt = np.ascontiguousarray(wg.T)
    wut = np.ascontiguousarray(wu.T)
    wdt = np.ascontiguousarray(wd.T)
    nrel = rel_bias.shape[0]
    ext = rel_bias[np.minimum(np.arange(2 * T - 1), nrel - 1)]
    idx = np.arange(128)[:, None] - np.arange(3968)[None, :] + 3967
    bandh = np.ascontiguousarray(ext[idx, :].transpose(2, 0, 1))
    identb = np.eye(128, dtype=np.float32).astype(ml_dtypes.bfloat16)
    onesqb = np.ones((128, 128), np.float32).astype(ml_dtypes.bfloat16)
    xt_all = np.ascontiguousarray(x.transpose(0, 2, 1))

    shared = dict(wqkt=wqkt, wvt=wvt, woutt=woutt, bandh=bandh,
                  wgt=wgt, wut=wut, wdt=wdt,
                  identi=identb, onesqi=onesqb)
    in_maps = []
    for c in range(NCORES):
        m = dict(shared)
        m["xt"] = np.ascontiguousarray(xt_all[c * SEQN:(c + 1) * SEQN])
        in_maps.append(m)
    return in_maps


def kernel(**inputs):
    in_maps = _host_prep(**inputs)
    nc = _get_nc()
    res = run_bass_kernel_spmd(nc, in_maps, core_ids=list(range(NCORES)))
    outs = []
    for c in range(NCORES):
        o = res.results[c]["outt"]
        outs.append(np.ascontiguousarray(np.transpose(o, (0, 2, 1))))
    return np.concatenate(outs, axis=0)


# version 39
# speedup vs baseline: 1.0074x; 1.0063x over previous
import contextlib
import numpy as np
import ml_dtypes
import concourse.bass as bass
import concourse.mybir as mybir
from concourse.bass_utils import run_bass_kernel_spmd

F32 = mybir.dt.float32
BF16 = mybir.dt.bfloat16
I32 = mybir.dt.int32
AF = mybir.ActivationFunctionType
ALU = mybir.AluOpType

NCORES = 8
SEQN = 2
T = 2048
D = 1024
H = 16
DH = 64
FF = 3072
NF = 8
INV_D = 1.0 / float(D)
EPS = 1e-5
EXP_SHIFT = -20.0
QSCALE = 0.125
NEG_LN127 = -float(np.log(127.0))

DMAQ = ("DSP", "DGP")
ENG_OF = {"PE": "tensor", "ACT": "scalar", "DVE": "vector",
          "DSP": "sync", "DGP": "gpsimd", "GP": "gpsimd"}


class Rec:

    def __init__(self):
        self.ops = []
        self.dbg = []
        self.cnt = {}
        self.last_w = {}
        self.readers = {}

    def op(self, q, emit, reads=(), writes=()):
        idx = self.cnt.get(q, 0)
        self.cnt[q] = idx + 1
        deps = set()
        for r in reads:
            w = self.last_w.get(r)
            if w is not None:
                deps.add(w)
        for r in writes:
            w = self.last_w.get(r)
            if w is not None:
                deps.add(w)
            rd = self.readers.get(r)
            if rd:
                deps.update(rd.items())
                self.readers[r] = {}
        for r in reads:
            rd = self.readers.setdefault(r, {})
            if rd.get(q, -1) < idx:
                rd[q] = idx
        for r in writes:
            self.last_w[r] = (q, idx)
        deps = frozenset(d for d in deps if d[0] != q)
        self.ops.append((q, emit, deps, idx))
        self.dbg.append((q, idx, tuple(reads), tuple(writes), deps))
        return (q, idx)


def _emit_program(nc, rec):
    with contextlib.ExitStack() as stk:
        sems = {}
        for q in ("PE", "ACT", "DVE", "DSP", "DGP"):
            sems[q] = stk.enter_context(nc.semaphore("sem_" + q))
        marked = {}
        for q, emit, deps, idx in rec.ops:
            for (q2, i2) in deps:
                if q2 not in DMAQ:
                    marked.setdefault(q2, set()).add(i2)
        rank = {q: {i: r + 1 for r, i in enumerate(sorted(s))}
                for q, s in marked.items()}
        per_eng = {"tensor": [], "scalar": [], "vector": [], "sync": [], "gpsimd": []}
        for rec_op in rec.ops:
            per_eng[ENG_OF[rec_op[0]]].append(rec_op)

        def mk_body(ops_list):
            def body(eng):
                waited = {}
                for q, emit, deps, idx in ops_list:
                    for (q2, i2) in sorted(deps):
                        if q2 in DMAQ:
                            val = 16 * (i2 + 1)
                        else:
                            val = rank[q2][i2]
                        if waited.get(q2, 0) < val:
                            eng.wait_ge(sems[q2], val)
                            waited[q2] = val
                    inst = emit(eng)
                    if q in DMAQ:
                        inst.then_inc(sems[q], 16)
                    elif idx in marked.get(q, ()):
                        inst.then_inc(sems[q], 1)
            return body

        block = stk.enter_context(nc.Block())
        block.tensor(mk_body(per_eng["tensor"]))
        block.scalar(mk_body(per_eng["scalar"]))
        block.vector(mk_body(per_eng["vector"]))
        block.sync(mk_body(per_eng["sync"]))
        block.gpsimd(mk_body(per_eng["gpsimd"]))


def _scr_keys(c0, c1):
    return tuple(f"S{i}" for i in range(c0 // 512, (c1 + 511) // 512))


R3K = tuple(f"R3:{h}" for h in range(H))
R4K = ("vaug", "w40", "w41")



def _wo_tile(fo, kc):
    if fo < 3:
        t = fo * 8 + kc
    elif fo == 3:
        t = 24 + kc if kc < 7 else 63
    else:
        t = 31 + (fo - 4) * 8 + kc
    return t * 128


def build_nc(taps=(), stop_after=None):
    nc = bass.Bass("TRN2", target_bir_lowering=False)
    xt = nc.dram_tensor("xt", [SEQN, D, T], F32, kind="ExternalInput").ap()
    wqkt = nc.dram_tensor("wqkt", [D, 2048], F32, kind="ExternalInput").ap()
    wvt = nc.dram_tensor("wvt", [D, D], F32, kind="ExternalInput").ap()
    woutt = nc.dram_tensor("woutt", [128, 8192], F32, kind="ExternalInput").ap()
    bandh = nc.dram_tensor("bandh", [H, 128, 3968], F32, kind="ExternalInput").ap()
    wgt = nc.dram_tensor("wgt", [D, FF], I32, kind="ExternalInput").ap()
    wut = nc.dram_tensor("wut", [D, FF], I32, kind="ExternalInput").ap()
    wdt = nc.dram_tensor("wdt", [FF, D], I32, kind="ExternalInput").ap()
    identi = nc.dram_tensor("identi", [128, 128], BF16, kind="ExternalInput").ap()
    onesqi = nc.dram_tensor("onesqi", [128, 128], BF16, kind="ExternalInput").ap()
    outt = nc.dram_tensor("outt", [SEQN, D, T], F32, kind="ExternalOutput").ap()
    tap_out = {}
    for tname, shape, dt in taps:
        tap_out[tname] = nc.dram_tensor("tap_" + tname, shape, dt,
                                        kind="ExternalOutput").ap()

    rec = Rec()
    O = rec.op
    with contextlib.ExitStack() as stk:
        E = stk.enter_context
        R12 = E(nc.sbuf_tensor("R12", [128, 16384], F32))
        R3 = E(nc.sbuf_tensor("R3", [128, 16384], BF16))
        R4 = E(nc.sbuf_tensor("R4", [128, 24832], BF16))
        R5 = E(nc.sbuf_tensor("R5", [128, 4096], F32))
        SCR = E(nc.sbuf_tensor("SCR", [128, 14336], BF16))
        Rsb = E(nc.sbuf_tensor("Rsb", [64, 1024], BF16))
        pvo = E(nc.sbuf_tensor("pvo", [64, 1024], BF16))
        rrow = E(nc.sbuf_tensor("rrow", [128, 1024], BF16))
        lkmrow = E(nc.sbuf_tensor("lkmrow", [1, 8], F32))
        rows_f = E(nc.sbuf_tensor("rows_f", [1, 1536], F32))
        rows_b = E(nc.sbuf_tensor("rows_b", [1, 2048], BF16))
        ident = E(nc.sbuf_tensor("ident", [128, 128], BF16))
        onesq = E(nc.sbuf_tensor("onesq", [128, 128], BF16))
        mz = E(nc.sbuf_tensor("mz", [128, 1024], BF16))
        cpool = E(nc.sbuf_tensor("cpool", [128, 8], F32))
        rot = E(nc.psum_tensor("rot", [128, 2048], F32))
        pv = E(nc.psum_tensor("pv", [128, 512], F32))
        rbc = E(nc.psum_tensor("rbc", [128, 512], F32))
        st1 = E(nc.psum_tensor("st1", [128, 512], F32))
        st2 = E(nc.psum_tensor("st2", [128, 512], F32))

        R12b = R12[:].bitcast(BF16)
        qf = R12b[:, 0:16384]
        kf = R12b[:, 16384:32768]
        x1 = R12[:]
        x1_3d = x1.rearrange("p (c n) -> p c n", c=NF, n=T)
        xhat = R3[:]
        attnout = R3[:]
        ybF = R3[:, 0:8192]
        vaug3 = R4[:, 0:16640].rearrange("p (b e) -> p b e", e=65)
        wslot = R4[:, 16640:24832]
        wv3 = wslot.rearrange("p (c n) -> p c n", c=NF, n=1024)
        wo3 = wslot.rearrange("p (c n) -> p c n", c=NF, n=1024)
        hbuf = R4[:, 0:24576]
        xin = R5[:]
        xh2 = R5[:].bitcast(BF16)
        cp = lambda j: cpool[:, j:j + 1]
        cp1 = lambda j: cpool[0:1, j:j + 1]

        WS = [0, 1024, 2048, 3072]
        DD = [4096, 7168]
        SQ = [10240, 10752]
        TS = [11264, 11776]
        XB = [12288, 12800]
        SS = [13312, 13824]

        def scr(c0, w):
            return SCR[:, c0:c0 + w]

        def ms(ap_, v):
            return lambda eng, a=ap_, vv=v: eng.memset(a, vv)
        for j, v in ((0, EXP_SHIFT), (1, QSCALE), (2, EPS), (3, -0.5), (4, NEG_LN127),
                     (5, -1.0), (6, 1e-37), (7, 0.5)):
            O("DVE", ms(cpool[:, j:j + 1], v), writes=("cp",))
        O("DSP", lambda eng: eng.dma_start(out=ident[:], in_=identi), writes=("ident",))
        O("DSP", lambda eng: eng.dma_start(out=onesq[:], in_=onesqi), writes=("onesq",))
        O("DVE", lambda eng: eng.memset(mz[:], 0.0), writes=("mz0", "mz1"))
        O("DVE", lambda eng: eng.memset(rrow[:], 0.0), writes=("rr0", "rr1"))
        O("DVE", lambda eng: eng.memset(SCR[:, 12288:14336], 0.0),
          writes=_scr_keys(12288, 14336))

        rotk = [f"rot{i}" for i in range(4)]

        def mm(out_ap, lhsT, rhs, start, stop, reads, writes):
            return O("PE", lambda eng, o=out_ap, l=lhsT, r=rhs, s1=start, s2=stop:
                     eng.matmul(o, l, r, start=s1, stop=s2),
                     reads=reads, writes=writes)

        def stats_rows(par, width, scale_j=None):
            m = rows_f[0:1, par * 512: par * 512 + width]
            var = rows_f[0:1, 1024:1024 + width]
            a = rows_b[0:1, par * 512: par * 512 + width]
            nb = rows_b[0:1, 1024 + par * 512: 1024 + par * 512 + width]
            mk = f"RF{par}"
            O("DVE", lambda eng: eng.tensor_scalar_mul(m, st1[0:1, 0:width], INV_D),
              reads=("st1",), writes=(mk,))
            O("DVE", lambda eng: eng.tensor_tensor(var, m, m, ALU.mult),
              reads=(mk,), writes=("RF2",))
            O("DVE", lambda eng: eng.scalar_tensor_tensor(
                var, st2[0:1, 0:width], INV_D, var, ALU.mult, ALU.subtract),
              reads=("st2", "RF2"), writes=("RF2",))
            O("ACT", lambda eng: eng.activation(var, var, AF.Ln, bias=cp1(2)),
              reads=("RF2", "cp"), writes=("RF2",))
            bias_arg = cp1(scale_j) if scale_j is not None else 0.0
            O("ACT", lambda eng, b=bias_arg: eng.activation(
                a, var, AF.Exp, bias=b, scale=cp1(3)),
              reads=("RF2", "cp"), writes=(f"ra{par}",))
            O("DVE", lambda eng: eng.tensor_scalar_mul(nb, m, -1.0),
              reads=(mk,), writes=(f"rb{par}",))
            return a, nb

        def bcast(dst_ap, row_ap, rowkey, dstkey, width):
            mm(dst_ap, onesq[0:1, 0:128], row_ap, True, True,
               ("onesq", rowkey), (dstkey,))

        def stats_pair(nkeys, st1_ap, st2_ap, xb_ap, sq_ap, xbk, sqk, first, last):
            mm(st1_ap, onesq[:, 0:1], xb_ap, first, last, ("onesq", xbk), ("st1",))
            mm(st2_ap, onesq[:, 0:1], sq_ap, first, last, ("onesq", sqk), ("st2",))

        def stage_dma(dst_ap, src_ap, wkeys):
            return O("DGP", lambda eng, d=dst_ap, s=src_ap:
                     eng.dma_start(out=d, in_=s), writes=wkeys)

        rotc = [0]
        carryF = [None]

        def next_rot():
            i = rotc[0] % 4
            rotc[0] += 1
            return rot[:, i * 512:(i + 1) * 512], rotk[i]

        out_dma_tokens = []

        for s in range(SEQN):
            stage_dma(wv3, wvt.rearrange("(c p) n -> p c n", p=128), ("w40", "w41"))
            O("DVE", lambda eng: eng.memset(vaug3[:, :, 64:65], 1.0), writes=("vaug",))

            for c8 in range(8):
                p5 = c8 % 2
                xin_h = xin[:, p5 * 2048:(p5 + 1) * 2048]
                xin_3 = xin_h.rearrange("p (c n) -> p c n", c=NF, n=256)
                src = xt[s, :, c8 * 256:(c8 + 1) * 256].rearrange(
                    "(c p) n -> p c n", p=128)
                O("DSP", lambda eng, d=xin_3, ss=src: eng.dma_start(out=d, in_=ss),
                  writes=tuple(f"XH{p5 * 4 + j}" for j in range(4)))
                for f in range(NF):
                    fp = f % 2
                    xb = scr(XB[fp], 256)
                    sq = scr(SQ[fp], 256)
                    xf = xin_3[:, f, :]
                    O("ACT", lambda eng, a=xb, b=xf: eng.activation(a, b, AF.Copy),
                      reads=(f"XH{p5 * 4 + f // 2}",), writes=(_scr_keys(XB[fp], XB[fp] + 512)[0],))
                    O("ACT", lambda eng, a=sq, b=xf: eng.activation(a, b, AF.Square),
                      reads=(f"XH{p5 * 4 + f // 2}",), writes=(_scr_keys(SQ[fp], SQ[fp] + 512)[0],))
                    stats_pair(None, st1[0:1, 0:256], st2[0:1, 0:256], xb, sq,
                               _scr_keys(XB[fp], XB[fp] + 512)[0], _scr_keys(SQ[fp], SQ[fp] + 512)[0], f == 0, f == NF - 1)
                par = c8 % 2
                a, nb = stats_rows(par, 256)
                if carryF[0] is not None:
                    carryF[0]()
                    carryF[0] = None
                if c8 % 2 == 0:
                    A_ps, Ak = rot[:, 0:256], rotk[0]
                    B_ps, Bk = rot[:, 512:768], rotk[1]
                else:
                    A_ps, Ak = pv[:, 0:256], "pv"
                    B_ps, Bk = rbc[:, 0:256], "rbc"
                bcast(A_ps, a, f"ra{par}", Ak, 256)
                bcast(B_ps, nb, f"rb{par}", Bk, 256)
                for f in range(NF):
                    tsl = scr(TS[f % 2], 256)
                    xf = xin_3[:, f, :]
                    xo = xhat[:, f * T + c8 * 256: f * T + (c8 + 1) * 256]
                    O("DVE", lambda eng, o=tsl, i0=xf, i1=B_ps:
                      eng.tensor_tensor(o, i0, i1, ALU.add),
                      reads=(f"XH{p5 * 4 + f // 2}", Bk), writes=(_scr_keys(TS[f % 2], TS[f % 2] + 512)[0],))
                    O("DVE", lambda eng, o=xo, i0=tsl, i1=A_ps:
                      eng.tensor_tensor(o, i0, i1, ALU.mult),
                      reads=(_scr_keys(TS[f % 2], TS[f % 2] + 512)[0], Ak), writes=R3K)

            if stop_after == "A0":
                break
            PT4 = [4096, 5120]
            NSQ = [2048, 2048]
            ES = [0, 1024]
            LQ0 = 6144
            MROW0 = 8192

            def norm_steps(h, nsq=2048):
                hc, rb = h // 2, (h % 2) * 64
                par = h % 2
                lq = SCR[0:1, LQ0:LQ0 + 2048]
                pbm = MROW0 + par * 2048
                lk = SCR[0:1, pbm:pbm + 2048]
                mrow_full = SCR[0:1, pbm:pbm + 2048]
                lkm = lkmrow[0:1, 0:1]
                halfkm = lkmrow[0:1, 1:2]
                lqk = _scr_keys(LQ0, LQ0 + 2048)
                lkk = _scr_keys(pbm, pbm + 2048)
                steps = []
                for src_f, srck, outrow, outk in (
                        (qf, f"QK{hc}", lq, lqk),
                        (kf, f"QK{8 + hc}", lk, lkk)):
                    sq_ap = SCR[rb:rb + 64, nsq:nsq + 2048]
                    blk = src_f[:, hc * T: (hc + 1) * T][rb:rb + 64, :]
                    steps.append(lambda sq_ap=sq_ap, blk=blk, srck=srck: O(
                        "DVE", lambda eng, o=sq_ap, i=blk: eng.tensor_tensor(
                            o, i, i, ALU.mult),
                        reads=(srck,), writes=_scr_keys(nsq, nsq + 2048)))
                    for c in range(4):
                        def step(c=c, outrow=outrow, outk=outk, rb=rb):
                            mm(st1[0:1, 0:512], onesq[rb:rb + 64, 0:1],
                               SCR[rb:rb + 64, nsq + c * 512: nsq + (c + 1) * 512],
                               True, True,
                               ("onesq",) + _scr_keys(nsq, nsq + 2048),
                               ("st1",))
                            O("ACT", lambda eng, o=outrow[0:1, c * 512:(c + 1) * 512]:
                              eng.activation(o, st1[0:1, 0:512], AF.Ln),
                              reads=("st1",), writes=outk)
                        steps.append(step)
                steps.append(lambda: O(
                    "DVE", lambda eng, o=lq, i=lq: eng.tensor_scalar_mul(o, i, 0.5),
                    reads=lqk, writes=lqk))
                steps.append(lambda: O(
                    "DVE", lambda eng, o=lk, i=lk: eng.tensor_scalar_mul(o, i, 0.5),
                    reads=lkk, writes=lkk))
                steps.append(lambda: O(
                    "DVE", lambda eng, o=halfkm: eng.reduce_max(
                        o, lk, axis=mybir.AxisListType.X),
                    reads=lkk, writes=("hkm",)))
                steps.append(lambda: O(
                    "ACT", lambda eng, o=mrow_full, i=lq: eng.activation(
                        o, i, AF.Exp, bias=halfkm),
                    reads=lqk + ("hkm",), writes=lkk))
                steps.append(lambda: O(
                    "DVE", lambda eng, o=mrow_full: eng.tensor_scalar_mul(o, o, -1.0),
                    reads=lkk, writes=lkk))
                return mrow_full, lkk, steps

            for kc in range(16):
                for nh in range(2):
                    rr, rk = next_rot()
                    for f in range(NF):
                        mm(rr, xhat[:, f * T + kc * 128: f * T + (kc + 1) * 128],
                           wv3[:, f, nh * 512:(nh + 1) * 512],
                           f == 0, f == NF - 1,
                           R3K + ("w40", "w41"), (rk,))
                    b0 = kc * 16 + 8 * nh
                    dst = vaug3[:, b0:b0 + 8, 0:64]
                    O("ACT", lambda eng, d=dst, ss=rr: eng.activation(d, ss, AF.Copy),
                      reads=(rk,), writes=("vaug",))

            def qk_stage(i):
                dst = scr(WS[i % 4], 1024).rearrange("p (c n) -> p c n", n=128)
                src = wqkt[:, i * 128:(i + 1) * 128].rearrange("(c p) n -> p c n", p=128)
                stage_dma(dst, src, (f"S{WS[i % 4] // 512}", f"S{WS[i % 4] // 512 + 1}"))

            qk_stage(0)
            qk_stage(1)
            h0p = [None, None, []]
            for i in range(16):
                wsk = (f"S{WS[i % 4] // 512}", f"S{WS[i % 4] // 512 + 1}")
                w3 = scr(WS[i % 4], 1024).rearrange("p (c n) -> p c n", n=128)
                for n in range(4):
                    rr, rk = next_rot()
                    for f in range(NF):
                        mm(rr, w3[:, f, :], xhat[:, f * T + n * 512: f * T + (n + 1) * 512],
                           f == 0, f == NF - 1, R3K + wsk, (rk,))
                    if i < 8:
                        dst = qf[:, i * T + n * 512: i * T + (n + 1) * 512]
                        O("ACT", lambda eng, d=dst, ss=rr: eng.activation(
                            d, ss, AF.Copy, scale=cp(1)),
                          reads=(rk, "cp"), writes=(f"QK{i}",))
                    else:
                        dst = kf[:, (i - 8) * T + n * 512: (i - 8) * T + (n + 1) * 512]
                        O("DVE", lambda eng, d=dst, ss=rr: eng.tensor_copy(d, ss),
                          reads=(rk,), writes=(f"QK{i}",))
                    if h0p[2]:
                        h0p[2].pop(0)()
                if i == 8:
                    h0p = list(norm_steps(0, nsq=4096))
                if i + 2 < 16:
                    qk_stage(i + 2)

            if stop_after == "A1":
                break
            O("DVE", lambda eng: eng.memset(SCR[:, 12288:14336], 0.0),
              writes=_scr_keys(12288, 14336))
            stage_dma(R4[:, 16640:16640 + 3968], bandh[0], ("band0", "w40", "w41"))
            O("ACT", lambda eng, o=R4[:, 16640:16640 + 3968]: eng.activation(
                o, o, AF.Exp), reads=("band0",), writes=("band0",))

            def emit_norms(h):
                mrow_full, lkk, steps = norm_steps(h)
                for f in steps:
                    f()
                return mrow_full, lkk

            iters = [(h, qt) for h in range(H) for qt in range(4)]
            for f in h0p[2]:
                f()
            h0p[2] = []
            mrow_of = {}
            mrow_of[0] = (h0p[0], h0p[1])

            def emit_qzmz(t):
                h, qt = iters[t]
                hc, rb = h // 2, (h % 2) * 64
                qsl = 12288 + ((2 if rb else 0) + t % 2) * 512
                qz = SCR[:, qsl:qsl + 512]
                qzk = _scr_keys(qsl, qsl + 512)
                q_src = qf[:, hc * T + qt * 512: hc * T + (qt + 1) * 512][rb:rb + 64, :]
                O("DVE", lambda eng, o=qz[rb:rb + 64, :], i=q_src: eng.tensor_copy(o, i),
                  reads=(f"QK{hc}",), writes=qzk)
                mrow_full, mrk = mrow_of[h]
                mzsl = (t % 2) * 512
                mzk = (f"mz{t % 2}",)
                mr = mrow_full[0:1, qt * 512:(qt + 1) * 512]
                O("DVE", lambda eng, o=mz[0:1, mzsl:mzsl + 512], i=mr: eng.tensor_copy(
                    o, i),
                  reads=mrk, writes=mzk)

            hookq = []

            def emit_P2(t, mid=None):
                h, qt = iters[t]
                hc, rb = h // 2, (h % 2) * 64
                bsl = h % 2
                bkey = f"band{bsl}"
                bnd = R4[:, 16640 + bsl * 3968: 16640 + bsl * 3968 + 3968]
                qsl = 12288 + ((2 if rb else 0) + t % 2) * 512
                qz = SCR[:, qsl:qsl + 512]
                qzk = _scr_keys(qsl, qsl + 512)
                mzsl = (t % 2) * 512
                mzk = (f"mz{t % 2}",)
                mzv = mz[:, mzsl:mzsl + 512]

                PSLOT = [0, 1024, 4096, 5120]

                def pv_pair(g):
                    base = PSLOT[g % 4]
                    for jj in range(2):
                        kc = 2 * g + jj
                        pt_ap = scr(base + jj * 512, 512)
                        mm(pv[0:65, :], vaug3[:, kc * 16 + h, :], pt_ap,
                           kc == 0, kc == 15,
                           ("vaug", f"S{(base + jj * 512) // 512}"), ("pv",))

                for g in range(8):
                    if g == 4 and mid is not None:
                        mid()
                    if hookq:
                        hookq.pop(0)()
                    for jj in range(2):
                        kc = 2 * g + jj
                        rsl = 2 * (g % 2) + jj
                        rr = rot[:, rsl * 512:(rsl + 1) * 512]
                        k_l = kf[:, hc * T + kc * 128: hc * T + (kc + 1) * 128]
                        mm(rr, k_l, qz, True, False, (f"QK{8 + hc}",) + qzk, (rotk[rsl],))
                        mm(rr, onesq[:], mzv, False, True,
                           ("onesq",) + mzk, (rotk[rsl],))
                    if g == 5 and t + 1 < NIT:
                        emit_qzmz(t + 1)
                    base = PSLOT[g % 4]
                    es_ap = scr(base, 1024)
                    esk = _scr_keys(base, base + 1024)
                    rpair = rot[:, (g % 2) * 1024:(g % 2 + 1) * 1024]
                    O("ACT", lambda eng, d=es_ap, ss=rpair: eng.activation(
                        d, ss, AF.Exp),
                      reads=(rotk[2 * (g % 2)], rotk[2 * (g % 2) + 1]), writes=esk)
                    for jj in range(2):
                        kc = 2 * g + jj
                        x0 = 1920 + qt * 512 - kc * 128
                        pt_ap = scr(base + jj * 512, 512)
                        ptk1 = (f"S{(base + jj * 512) // 512}",)
                        ev = bnd[:, x0:x0 + 512]
                        O("DVE", lambda eng, d=pt_ap, i1=ev:
                          eng.tensor_tensor(d, d, i1, ALU.mult),
                          reads=esk + (bkey,), writes=ptk1)
                    if g > 1:
                        pv_pair(g - 2)
                pv_pair(6)
                pv_pair(7)

            def emit_fin_a(t):
                h, qt = iters[t]
                par = t % 2
                rf = rows_f[0:1, 0:512]
                O("ACT", lambda eng: eng.activation(rf, pv[64:65, :], AF.Ln),
                  reads=("pv",), writes=("RF0",))
                O("ACT", lambda eng, o=rrow[0:1, par * 512:(par + 1) * 512]:
                  eng.activation(o, rf, AF.Exp, scale=cp1(5)),
                  reads=("RF0", "cp"), writes=(f"rr{par}",))
                O("DVE", lambda eng, o=pvo[:, par * 512:(par + 1) * 512]:
                  eng.tensor_copy(o, pv[0:64, :]),
                  reads=("pv",), writes=(f"pvo{par}",))

            def emit_fin_b(t):
                h, qt = iters[t]
                hc = h // 2
                par = t % 2
                mm(rbc[0:64, :], onesq[:, 0:64], rrow[:, par * 512:(par + 1) * 512],
                   True, True, ("onesq", f"rr{par}"), ("rbc",))
                Rs = Rsb[:, par * 512:(par + 1) * 512]
                O("DVE", lambda eng, o=Rs: eng.tensor_copy(o, rbc[0:64, :]),
                  reads=("rbc",), writes=(f"Rs{par}",))
                cols = hc * T + qt * 512
                pvs = pvo[:, par * 512:(par + 1) * 512]
                if h % 2 == 0:
                    dst = attnout[0:64, cols:cols + 512]
                    O("DVE", lambda eng, d=dst, i0=pvs, i1=Rs: eng.tensor_tensor(
                        d, i0, i1, ALU.mult),
                      reads=(f"pvo{par}", f"Rs{par}"), writes=(f"R3:{h}",))
                else:
                    O("DVE", lambda eng, d=pvs, i1=Rs: eng.tensor_tensor(
                        d, d, i1, ALU.mult),
                      reads=(f"pvo{par}", f"Rs{par}"), writes=(f"pvo{par}",))
                    dst = attnout[64:128, cols:cols + 512]
                    O("DSP", lambda eng, d=dst, ss=pvs: eng.dma_start(
                        out=d, in_=ss),
                      reads=(f"pvo{par}",), writes=(f"R3:{h}",))

            NIT = len(iters)
            emit_qzmz(0)
            for t in range(NIT):
                h, qt = iters[t]
                if t == NIT - 4 and H == 16:
                    stage_dma(wslot[:, 0:3968], woutt[:, 0:3968],
                              ("band0", "w40"))
                    srcv0 = xt[s, :, 0:512].rearrange("(c p) n -> p c n", p=128)
                    O("DSP", lambda eng, d=xin.rearrange(
                        "p (c n) -> p c n", c=NF, n=512), ss=srcv0:
                      eng.dma_start(out=d, in_=ss),
                      writes=tuple(f"XH{j}" for j in range(8)))
                if qt == 1 and h + 1 < H:
                    nsl = (h + 1) % 2
                    wk = ("band1", "w41") if nsl == 1 else ("band0", "w40")
                    bsl_ap = R4[:, 16640 + nsl * 3968:
                                16640 + nsl * 3968 + 3968]
                    stage_dma(bsl_ap, bandh[h + 1], wk)
                    mrow_full, lkk, steps = norm_steps(h + 1)
                    bsteps = []
                    for c4 in range(4):
                        bv = R4[:, 16640 + nsl * 3968 + c4 * 992:
                                16640 + nsl * 3968 + (c4 + 1) * 992]
                        bsteps.append(lambda bv=bv, wk=wk: O(
                            "ACT", lambda eng, o=bv: eng.activation(o, o, AF.Exp),
                            reads=wk[0:1], writes=wk[0:1]))
                    mrow_of[h + 1] = (mrow_full, lkk)
                    assert not hookq
                    hookq.extend(bsteps + steps)
                emit_P2(t, mid=(lambda tt=t - 1: emit_fin_b(tt)) if t >= 1 else None)
                emit_fin_a(t)
            assert not hookq
            emit_fin_b(NIT - 1)

            if stop_after == "B":
                break
            stage_dma(wslot[:, 3968:8192], woutt[:, 3968:8192],
                      ("band1", "w41"))
            xin_3 = xin.rearrange("p (c n) -> p c n", c=NF, n=512)

            def emit_xin_dma(qt):
                srcv = xt[s, :, qt * 512:(qt + 1) * 512].rearrange(
                    "(c p) n -> p c n", p=128)
                O("DSP", lambda eng, d=xin_3, ss=srcv: eng.dma_start(out=d, in_=ss),
                  writes=tuple(f"XH{j}" for j in range(8)))

            def make_post2(qt, yb2, yb2keys):
                par = qt % 2
                a, nb = stats_rows(par, 512)

                def post2():
                    A_ps, Ak = pv[:], "pv"
                    B_ps, Bk = rbc[:], "rbc"
                    bcast(A_ps, a, f"ra{par}", Ak, 512)
                    bcast(B_ps, nb, f"rb{par}", Bk, 512)
                    for fo in range(NF):
                        tk = _scr_keys(TS[fo % 2], TS[fo % 2] + 512)[0]
                        tsl = scr(TS[fo % 2], 512)
                        yfo = yb2[:, fo, :]
                        xi = xin_3[:, fo, :]
                        xo = x1_3d[:, fo, qt * 512:(qt + 1) * 512]
                        x1k = f"QK{2 * fo + qt // 2}"
                        O("DVE", lambda eng, o=tsl, i0=yfo, i1=B_ps:
                          eng.tensor_tensor(o, i0, i1, ALU.add),
                          reads=yb2keys + (Bk,), writes=(tk,))
                        O("DVE", lambda eng, o=tsl, i1=A_ps:
                          eng.tensor_tensor(o, o, i1, ALU.mult),
                          reads=(tk, Ak), writes=(tk,))
                        O("DVE", lambda eng, o=xo, i0=tsl, i1=xi:
                          eng.tensor_tensor(o, i0, i1, ALU.add),
                          reads=(tk, f"XH{fo}"),
                          writes=(x1k,))
                        if qt < 3:
                            xsrc = xt[s, fo * 128:(fo + 1) * 128,
                                      (qt + 1) * 512:(qt + 2) * 512]
                            O("DSP", lambda eng, d=xi, ss=xsrc:
                              eng.dma_start(out=d, in_=ss),
                              writes=(f"XH{fo}",))
                return post2

            post2p = [None]
            for qt in range(4):
                yb2base = (qt % 2) * 4096
                yb2 = scr(yb2base, 4096).rearrange("p (c n) -> p c n", c=NF, n=512)
                yb2keys = _scr_keys(yb2base, yb2base + 4096)
                pend2 = [None]
                for fo in range(NF):
                    rr, rk = next_rot()
                    for kc in range(NF):
                        woff = _wo_tile(fo, kc)
                        wk2 = ("band0", "w40") if woff < 3968 else ("band1", "w41")
                        mm(rr, wslot[:, woff:woff + 128],
                           attnout[:, kc * T + qt * 512: kc * T + (qt + 1) * 512],
                           kc == 0, kc == NF - 1,
                           wk2 + (f"R3:{2 * kc}", f"R3:{2 * kc + 1}"), (rk,))
                    yfo = yb2[:, fo, :]
                    O("ACT", lambda eng, d=yfo, ss=rr: eng.activation(d, ss, AF.Copy),
                      reads=(rk,), writes=yb2keys)
                    sqk = _scr_keys(SQ[fo % 2], SQ[fo % 2] + 512)[0]
                    sq = scr(SQ[fo % 2], 512)
                    O("ACT", lambda eng, a=sq, b=yfo: eng.activation(
                        a, b, AF.Square),
                      reads=yb2keys, writes=(sqk,))
                    if fo == 4 and post2p[0] is not None:
                        post2p[0]()
                        post2p[0] = None
                    if pend2[0] is not None:
                        pend2[0]()
                    def st_emit(fo=fo, yfo=yfo, sq=sq, sqk=sqk):
                        stats_pair(None, st1[0:1, :], st2[0:1, :], yfo, sq,
                                   yb2keys[fo], sqk, fo == 0, fo == NF - 1)
                    pend2[0] = st_emit
                pend2[0]()
                post2p[0] = make_post2(qt, yb2, yb2keys)

            if stop_after == "B2":
                if post2p[0] is not None:
                    post2p[0]()
                break

            def emit_ln2(tc):
                for nh2 in range(2):
                    coff = tc * 1024 + nh2 * 512
                    for f in range(NF):
                        fp = f % 2
                        xbk = _scr_keys(XB[fp], XB[fp] + 512)[0]
                        sqk = _scr_keys(SQ[fp], SQ[fp] + 512)[0]
                        xb = scr(XB[fp], 512)
                        sq = scr(SQ[fp], 512)
                        xf = x1_3d[:, f, coff:coff + 512]
                        x1k = f"QK{2 * f + tc}"
                        O("ACT", lambda eng, a=xb, b=xf: eng.activation(
                            a, b, AF.Copy), reads=(x1k,), writes=(xbk,))
                        O("ACT", lambda eng, a=sq, b=xf: eng.activation(
                            a, b, AF.Square), reads=(x1k,), writes=(sqk,))
                        stats_pair(None, st1[0:1, :], st2[0:1, :], xb, sq,
                                   xbk, sqk, f == 0, f == NF - 1)
                    par = nh2
                    a, nb = stats_rows(par, 512, scale_j=4)
                    A_ps, Ak = pv[:], "pv"
                    B_ps, Bk = rbc[:], "rbc"
                    bcast(A_ps, a, f"ra{par}", Ak, 512)
                    bcast(B_ps, nb, f"rb{par}", Bk, 512)
                    for f in range(NF):
                        xf = x1_3d[:, f, coff:coff + 512]
                        x1k = f"QK{2 * f + tc}"
                        xo = xh2[:, f * 1024 + nh2 * 512: f * 1024 + (nh2 + 1) * 512]
                        xok = f"XH{f}"
                        O("DVE", lambda eng, o=xo, i0=xf, i1=B_ps:
                          eng.tensor_tensor(o, i0, i1, ALU.add),
                          reads=(x1k, Bk), writes=(xok,))
                        O("DVE", lambda eng, o=xo, i1=A_ps:
                          eng.tensor_tensor(o, o, i1, ALU.mult),
                          reads=(xok, Ak), writes=(xok,))

            def gu_stage(g):
                for w_src, base in ((wgt, 0), (wut, 2)):
                    sl = WS[base + g % 2]
                    dst = scr(sl, 1024).rearrange("p (c n) -> p c n", n=128)
                    srcv = w_src[:, g * 128:(g + 1) * 128].rearrange(
                        "(c p) n -> p c n", p=128)
                    stage_dma(dst, srcv, _scr_keys(sl, sl + 1024))

            def wd_stage(fo):
                sl = DD[fo % 2]
                dst = scr(sl, 3072).rearrange("p (c n) -> p c n", n=128)
                srcv = wdt[:, fo * 128:(fo + 1) * 128].rearrange(
                    "(c p) n -> p c n", p=128)
                stage_dma(dst, srcv, _scr_keys(sl, sl + 3072))

            ybF3 = ybF.rearrange("p (c n) -> p c n", c=NF, n=1024)

            def emit_rowsF():
                for nh in range(2):
                    par = nh
                    m = rows_f[0:1, par * 512: par * 512 + 512]
                    var = rows_f[0:1, 1024:1536]
                    a = rrow[0:1, par * 512:(par + 1) * 512]
                    nb = Rsb[0:1, par * 512:(par + 1) * 512]
                    s1ap = st1[0:1, 0:512] if nh == 0 else st2[0:1, 0:512]
                    s2ap = pv[0:1, 0:512] if nh == 0 else rbc[0:1, 0:512]
                    s1k = "st1" if nh == 0 else "st2"
                    s2k = "pv" if nh == 0 else "rbc"
                    mk = f"RF{par}"
                    O("DVE", lambda eng, o=m, i=s1ap: eng.tensor_scalar_mul(
                        o, i, INV_D), reads=(s1k,), writes=(mk,))
                    O("DVE", lambda eng, o=var, i=m: eng.tensor_tensor(
                        o, i, i, ALU.mult), reads=(mk,), writes=("RF2",))
                    O("DVE", lambda eng, o=var, i0=s2ap, i1=var:
                      eng.scalar_tensor_tensor(o, i0, INV_D, i1, ALU.mult,
                                               ALU.subtract),
                      reads=(s2k, "RF2"), writes=("RF2",))
                    O("ACT", lambda eng, o=var: eng.activation(
                        o, o, AF.Ln, bias=cp1(2)), reads=("RF2", "cp"),
                      writes=("RF2",))
                    O("ACT", lambda eng, o=a, i=var: eng.activation(
                        o, i, AF.Exp, scale=cp1(3)), reads=("RF2", "cp"),
                      writes=(f"rr{par}",))
                    O("DVE", lambda eng, o=nb, i=m: eng.tensor_scalar_mul(
                        o, i, -1.0), reads=(mk,), writes=(f"Rs{par}",))

            def make_postF(tc, s=s):
                def postF():
                    for nh in range(2):
                        par = nh
                        a = rrow[0:1, par * 512:(par + 1) * 512]
                        nb = Rsb[0:1, par * 512:(par + 1) * 512]
                        if tc == 0:
                            A_ps, Ak = st1[:, 0:512], "st1"
                            B_ps, Bk = st2[:, 0:512], "st2"
                        else:
                            A_ps, Ak = pv[:], "pv"
                            B_ps, Bk = rbc[:], "rbc"
                        bcast(A_ps, a, f"rr{par}", Ak, 512)
                        bcast(B_ps, nb, f"Rs{par}", Bk, 512)
                        for fo in range(NF):
                            tk = _scr_keys(TS[fo % 2], TS[fo % 2] + 512)[0]
                            tsl = scr(TS[fo % 2], 512)
                            yfo = ybF3[:, fo, nh * 512:(nh + 1) * 512]
                            xo = x1_3d[:, fo,
                                       tc * 1024 + nh * 512: tc * 1024 + (nh + 1) * 512]
                            x1k = f"QK{2 * fo + tc}"
                            O("DVE", lambda eng, o=tsl, i0=yfo, i1=B_ps:
                              eng.tensor_tensor(o, i0, i1, ALU.add),
                              reads=R3K + (Bk,), writes=(tk,))
                            O("DVE", lambda eng, o=tsl, i1=A_ps:
                              eng.tensor_tensor(o, o, i1, ALU.mult),
                              reads=(tk, Ak), writes=(tk,))
                            O("DVE", lambda eng, o=xo, i0=tsl:
                              eng.tensor_tensor(o, i0, o, ALU.add),
                              reads=(tk, x1k), writes=(x1k,))
                            coff = tc * 1024 + nh * 512
                            srcv = x1_3d[:, fo, coff:coff + 512]
                            dstv = outt[s].rearrange(
                                "(c p) n -> p c n",
                                p=128)[:, fo, coff:coff + 512]
                            tok = O("DSP", lambda eng, d=dstv, ss=srcv:
                                    eng.dma_start(out=d, in_=ss),
                                    reads=(x1k,))
                            out_dma_tokens.append(tok)
                return postF

            postFp = [None]
            if post2p[0] is not None:
                post2p[0]()
                post2p[0] = None
            for tc in range(2):
                if tc == 0:
                    emit_ln2(0)
                gu_stage(0)
                gu_stage(1)
                for g in range(24):
                    wg3 = scr(WS[g % 2], 1024).rearrange("p (c n) -> p c n", n=128)
                    wu3 = scr(WS[2 + g % 2], 1024).rearrange("p (c n) -> p c n", n=128)
                    wgk = _scr_keys(WS[g % 2], WS[g % 2] + 1024)
                    wuk = _scr_keys(WS[2 + g % 2], WS[2 + g % 2] + 1024)
                    if g == 4 and postFp[0] is not None:
                        postFp[0]()
                        postFp[0] = None
                    for nh in range(2):
                        rg, rgk = next_rot()
                        ru, ruk = next_rot()
                        for f in range(NF):
                            xr = xh2[:, f * 1024 + nh * 512: f * 1024 + (nh + 1) * 512]
                            xrk = (f"XH{f}",)
                            mm(rg, wg3[:, f, :], xr, f == 0, f == NF - 1,
                               wgk + xrk, (rgk,))
                        for f in range(NF):
                            xr = xh2[:, f * 1024 + nh * 512: f * 1024 + (nh + 1) * 512]
                            xrk = (f"XH{f}",)
                            mm(ru, wu3[:, f, :], xr, f == 0, f == NF - 1,
                               wuk + xrk, (ruk,))
                        ssk = _scr_keys(SS[nh], SS[nh] + 512)[0]
                        ssl = scr(SS[nh], 512)
                        O("ACT", lambda eng, d=ssl, ss=rg: eng.activation(
                            d, ss, AF.Silu),
                          reads=(rgk,), writes=(ssk,))
                        hdst = hbuf[:, g * 1024 + nh * 512: g * 1024 + (nh + 1) * 512]
                        O("DVE", lambda eng, d=hdst, i0=ssl, i1=ru:
                          eng.tensor_tensor(d, i0, i1, ALU.mult),
                          reads=(ssk, ruk), writes=R4K)
                    if g + 2 < 24:
                        gu_stage(g + 2)

                wd_stage(0)
                wd_stage(1)
                pend3 = [None]
                for fo in range(NF):
                    wd3 = scr(DD[fo % 2], 3072).rearrange("p (c n) -> p c n", n=128)
                    wdk = _scr_keys(DD[fo % 2], DD[fo % 2] + 3072)
                    for nh in range(2):
                        rr, rk = next_rot()
                        for kc in range(24):
                            hsrc = hbuf[:, kc * 1024 + nh * 512: kc * 1024 + (nh + 1) * 512]
                            mm(rr, wd3[:, kc, :], hsrc, kc == 0, kc == 23,
                               wdk + R4K, (rk,))
                        yfo = ybF3[:, fo, nh * 512:(nh + 1) * 512]
                        O("ACT", lambda eng, d=yfo, ss=rr: eng.activation(
                            d, ss, AF.Copy),
                          reads=(rk,), writes=R3K)
                        sqk = _scr_keys(SQ[nh], SQ[nh] + 512)[0]
                        sq = scr(SQ[nh], 512)
                        O("ACT", lambda eng, a=sq, b=yfo: eng.activation(
                            a, b, AF.Square),
                          reads=R3K, writes=(sqk,))
                        if pend3[0] is not None:
                            pend3[0]()
                        def st_emit(fo=fo, nh=nh, yfo=yfo, sq=sq, sqk=sqk):
                            s1ap = st1[0:1, :] if nh == 0 else st2[0:1, :]
                            s2ap = pv[0:1, 0:512] if nh == 0 else rbc[0:1, 0:512]
                            s1k = "st1" if nh == 0 else "st2"
                            s2k = "pv" if nh == 0 else "rbc"
                            mm(s1ap, onesq[:, 0:1], yfo, fo == 0, fo == NF - 1,
                               ("onesq",) + R3K, (s1k,))
                            mm(s2ap, onesq[:, 0:1], sq, fo == 0, fo == NF - 1,
                               ("onesq", sqk), (s2k,))
                        pend3[0] = st_emit
                    if fo + 2 < NF:
                        wd_stage(fo + 2)
                pend3[0]()
                emit_rowsF()
                if tc + 1 < 2:
                    emit_ln2(tc + 1)
                postFp[0] = make_postF(tc)
            carryF[0] = postFp[0]

        if carryF[0] is not None:
            carryF[0]()
            carryF[0] = None

        tapmap = {"xhat": (R3[:], BF16), "qf": (qf, BF16), "kf": (kf, BF16),
                  "vaug": (R4[:, 0:16640], BF16), "attnout": (R3[:], BF16),
                  "x1": (x1, F32), "xh2": (xh2, BF16), "hbuf": (hbuf, BF16), "scr": (SCR[:], BF16)}
        for tname in tap_out:
            if tname not in tapmap:
                continue
            src_ap, _dt = tapmap[tname]
            O("DSP", lambda eng, d=tap_out[tname], ss=src_ap: eng.dma_start(
                out=d, in_=ss),
              reads=tuple(f"QK{i}" for i in range(16)) + R3K + R4K + tuple(f"XH{j}" for j in range(8)))

        def fin(eng):
            return eng.engine_nop()
        O("GP", fin, reads=tuple(f"QK{i}" for i in range(16)))
        rec.ops[-1] = (rec.ops[-1][0], rec.ops[-1][1],
                       frozenset(out_dma_tokens), rec.ops[-1][3])

        _emit_program(nc, rec)
    build_nc.last_rec = rec
    return nc


_NC_CACHE = {}


def _get_nc(taps=()):
    key = tuple(taps)
    if key not in _NC_CACHE:
        _NC_CACHE[key] = build_nc(taps)
    return _NC_CACHE[key]


def _host_prep(L0_attn_x, L0_attn_w_qkv, L0_attn_w_out, L0_attn_rel_bias,
               L0_ffn_w_gate, L0_ffn_w_up, L0_ffn_w_down):
    x = np.asarray(L0_attn_x, dtype=np.float32)
    w_qkv = np.asarray(L0_attn_w_qkv, dtype=np.float32)
    w_out = np.asarray(L0_attn_w_out, dtype=np.float32)
    rel_bias = np.asarray(L0_attn_rel_bias, dtype=np.float32)
    wg = np.asarray(L0_ffn_w_gate, dtype=np.int32)
    wu = np.asarray(L0_ffn_w_up, dtype=np.int32)
    wd = np.asarray(L0_ffn_w_down, dtype=np.int32)

    hh, dd = np.meshgrid(np.arange(H), np.arange(DH), indexing="ij")
    perm_q = (48 * dd + 0 * 16 + hh).reshape(-1)
    perm_k = (48 * dd + 1 * 16 + hh).reshape(-1)
    perm_v = (48 * dd + 2 * 16 + hh).reshape(-1)
    wqkt = np.ascontiguousarray(w_qkv[np.concatenate([perm_q, perm_k])].T)
    wvt = np.ascontiguousarray(w_qkv[perm_v].T)
    wo_t = w_out.T
    woutt = np.empty((128, 8192), np.float32)
    for fo in range(8):
        for kc in range(8):
            t0 = _wo_tile(fo, kc)
            woutt[:, t0:t0 + 128] = wo_t[kc * 128:(kc + 1) * 128,
                                         fo * 128:(fo + 1) * 128]
    wgt = np.ascontiguousarray(wg.T)
    wut = np.ascontiguousarray(wu.T)
    wdt = np.ascontiguousarray(wd.T)
    nrel = rel_bias.shape[0]
    ext = rel_bias[np.minimum(np.arange(2 * T - 1), nrel - 1)]
    idx = np.arange(128)[:, None] - np.arange(3968)[None, :] + 3967
    bandh = np.ascontiguousarray(ext[idx, :].transpose(2, 0, 1))
    identb = np.eye(128, dtype=np.float32).astype(ml_dtypes.bfloat16)
    onesqb = np.ones((128, 128), np.float32).astype(ml_dtypes.bfloat16)
    xt_all = np.ascontiguousarray(x.transpose(0, 2, 1))

    shared = dict(wqkt=wqkt, wvt=wvt, woutt=woutt, bandh=bandh,
                  wgt=wgt, wut=wut, wdt=wdt,
                  identi=identb, onesqi=onesqb)
    in_maps = []
    for c in range(NCORES):
        m = dict(shared)
        m["xt"] = np.ascontiguousarray(xt_all[c * SEQN:(c + 1) * SEQN])
        in_maps.append(m)
    return in_maps


def kernel(**inputs):
    in_maps = _host_prep(**inputs)
    nc = _get_nc()
    res = run_bass_kernel_spmd(nc, in_maps, core_ids=list(range(NCORES)))
    outs = []
    for c in range(NCORES):
        o = res.results[c]["outt"]
        outs.append(np.ascontiguousarray(np.transpose(o, (0, 2, 1))))
    return np.concatenate(outs, axis=0)
